# version 51
# baseline (speedup 1.0000x reference)
from contextlib import ExitStack

import numpy as np

import concourse.bass as bass
import concourse.mybir as mybir
from concourse.bass_utils import run_bass_kernel_spmd

F32 = mybir.dt.float32
BF16 = mybir.dt.bfloat16
AF = mybir.ActivationFunctionType
ALU = mybir.AluOpType
AX = mybir.AxisListType

MAGIC = float(np.float32(3 * 2 ** 22))
D = 1024
HC = D // 128
I3 = 3072
IC = I3 // 128
QMAX = 32767.0
WQMAX = 7.0
EPS = 1e-5
N_CORES = 8


class Prog:
    ENGS = ("pe", "dv", "ac", "gp", "sp")

    def __init__(self, nc, es):
        self.nc = nc
        self.es = es
        self.sem = {}
        self.cnt = {}
        for name in ("pe", "dv", "ac", "gp"):
            self.getsem(name)
        self.waited = {e: {} for e in self.ENGS}
        self.buf = {}
        self.q = {e: [] for e in self.ENGS}

    def getsem(self, name):
        if name not in self.sem:
            self.sem[name] = self.es.enter_context(
                self.nc.semaphore("sem_" + name))
            self.cnt[name] = 0
        return self.sem[name]

    def _deps(self, reads, writes):
        toks = []
        for b in reads:
            st = self.buf.get(b)
            if st and st["w"]:
                toks.append(st["w"])
        for b in writes:
            st = self.buf.get(b)
            if st:
                if st["w"]:
                    toks.append(st["w"])
                toks.extend(st["r"])
        return toks

    def _waits_for(self, ename, toks, extra=None):
        w = self.waited[ename]
        need = {}
        for (s, v) in toks:
            if v > w.get(s, 0) and v > need.get(s, 0):
                need[s] = v
        if extra is not None:
            s, v = extra
            if v > w.get(s, 0) and v > need.get(s, 0):
                need[s] = v
        out = sorted(need.items())
        for s, v in out:
            w[s] = v
        return out

    def _record(self, ename, fns, reads, writes, incsem, incn, selfwait=True):
        toks = self._deps(reads, writes)
        extra = None
        if selfwait and incsem is not None and self.cnt[incsem] > 0:
            extra = (incsem, self.cnt[incsem])
        waits = self._waits_for(ename, toks, extra)
        token = None
        if incsem is not None:
            self.cnt[incsem] += incn
            token = (incsem, self.cnt[incsem])
        self.q[ename].append((waits, fns, incsem, incn))
        for b in reads:
            st = self.buf.setdefault(b, {"w": None, "r": []})
            if token:
                st["r"].append(token)
        for b in writes:
            self.buf[b] = {"w": token, "r": []}
        return token

    def op(self, ename, fn, reads=(), writes=()):
        return self._record(ename, [fn], reads, writes, ename, 1)

    def chain(self, ename, fns, reads=(), writes=()):
        return self._record(ename, fns, reads, writes, ename, 1)

    def dma(self, qname, ename, out, in_, reads=(), writes=()):
        self.getsem(qname)
        fn = (lambda e, o=out, i=in_: e.dma_start(out=o, in_=i))
        return self._record(ename, [fn], reads, writes, qname, 16,
                            selfwait=False)

    def fence(self):
        toks = [(s, self.cnt[s]) for s in self.sem if self.cnt[s] > 0]
        for e in self.ENGS:
            waits = self._waits_for(e, toks)
            if waits:
                self.q[e].append((waits, [], None, 0))

    def final_wait(self, ename, qname):
        self.q[ename].append(([(qname, self.cnt[qname])], [], None, 0))

    def replay(self, ename, eng):
        for waits, fns, incsem, incn in self.q[ename]:
            for s, v in waits:
                eng.wait_ge(self.sem[s], v)
            insts = [fn(eng) for fn in fns]
            if incsem is not None and insts:
                insts[-1].then_inc(self.sem[incsem], incn)


def build(T, world):
    nT = T // 128
    STW = min(T, 512)
    nST = max(T // 512, 1)
    rg = [list(range(world))]
    nc = bass.Bass("TRN2", target_bir_lowering=False)
    es = ExitStack()

    x_in = nc.dram_tensor("x", [T, D], F32, kind="ExternalInput").ap()
    wq_h = nc.dram_tensor("wq", [D, D], F32, kind="ExternalInput").ap()
    wk_h = nc.dram_tensor("wk", [D, D], F32, kind="ExternalInput").ap()
    wv_h = nc.dram_tensor("wv", [D, D], F32, kind="ExternalInput").ap()
    wo_h = nc.dram_tensor("wo", [D, D], F32, kind="ExternalInput").ap()
    bo_h = nc.dram_tensor("bo", [1, D], F32, kind="ExternalInput").ap()
    wg_h = nc.dram_tensor("wgate", [I3, D], F32, kind="ExternalInput").ap()
    wu_h = nc.dram_tensor("wup", [I3, D], F32, kind="ExternalInput").ap()
    wd_h = nc.dram_tensor("wdown", [D, I3], F32, kind="ExternalInput").ap()
    ident_h = nc.dram_tensor("ident", [128, 128], F32, kind="ExternalInput").ap()
    onesc_h = nc.dram_tensor("onesc", [128, 1], F32, kind="ExternalInput").ap()
    onesr_h = nc.dram_tensor("onesr", [1, 128], F32, kind="ExternalInput").ap()
    y_out = nc.dram_tensor("y", [T, D], F32, kind="ExternalOutput").ap()

    sp_up = nc.dram_tensor("sp_up", [IC, 128, T], F32, kind="Internal").ap()
    sp_gate = nc.dram_tensor("sp_gate", [IC, 128, T], F32, kind="Internal").ap()
    sp_g2 = nc.dram_tensor("sp_g2", [IC, 128, T], F32, kind="Internal").ap()
    sp_o = nc.dram_tensor("sp_o", [IC, 128, T], F32, kind="Internal").ap()
    sp_x2 = nc.dram_tensor("sp_x2", [128, nT, D], F32, kind="Internal").ap()
    sp_wdq = nc.dram_tensor("sp_wdq", [HC * (I3 // D), 128, D], F32,
                            kind="Internal").ap()
    cc_ctx_i = nc.dram_tensor("cc_ctx_i", [128, HC * 1025], F32, kind="Internal").ap()
    cc_ctx_o = nc.dram_tensor("cc_ctx_o", [128, HC * 1025], F32, kind="Internal").ap()
    cc_i = [nc.dram_tensor("cc_i%d" % k, [1, 2], F32, kind="Internal").ap()
            for k in range(4)]
    cc_o = [nc.dram_tensor("cc_o%d" % k, [1, 2], F32, kind="Internal").ap()
            for k in range(4)]

    A = lambda n, s: nc.alloc_sbuf_tensor(n, s, F32).ap()
    ident = A("c_ident", [128, 128])
    onesc = A("c_onesc", [128, 1])
    onesr = A("c_onesr", [1, 128])
    epsc = A("c_eps", [128, 1])
    ln32 = A("c_ln32", [1, 1])
    bo_bc = A("bo_bc", [128, D])
    xts = [A("xt0", [128, D]), A("xt1", [128, D]),
           A("xt2", [128, D]), A("xt3", [128, D])]
    hs = [A("h0", [128, D]), A("h1", [128, D])]
    h = hs[0]
    x2ts = [A("x2t0", [128, D]), A("x2t1", [128, D])]
    x2t = x2ts[0]
    sq_scr = A("sq_scr", [128, 512])
    stats = A("stats", [128, 2 * nT])
    gmax = A("gmax", [128, 8])
    scal = A("scal", [128, 16])
    m1x1 = A("m1x1", [1, 16])
    negc = A("negc", [128, 1])
    negcq = A("negcq", [128, 1])
    t128a = A("t_mean", [128, 1])
    t128b = A("t_inv", [128, 1])
    t128c = A("t_c", [128, 1])
    t128d = A("t_d", [128, 1])
    t128e = A("t_e", [128, 1])
    u128a = A("u_mean", [128, 1])
    u128b = A("u_inv", [128, 1])
    v128a = A("v_a", [128, 1])
    v128b = A("v_b", [128, 1])
    rs8 = A("rs8", [128, 8])
    blk = A("blk", [128, 32])
    blk2 = A("blk2", [128, 32])
    blk3 = A("blk3", [128, 32])
    srow = A("srow", [1, 128])
    wblkA = [A("wblkA0", [128, 32]), A("wblkA1", [128, 32])]
    wblkB = [A("wblkB0", [128, 32]), A("wblkB1", [128, 32])]
    wblkC = [A("wblkC0", [128, 32]), A("wblkC1", [128, 32])]
    ln6 = A("ln6", [128, 12])
    lns2 = A("lns2", [128, 2])
    onesc_bf = nc.alloc_sbuf_tensor("c_onesc_bf", [128, 1], BF16).ap()
    s1x1 = A("s1x1", [1, 8])

    S0 = ((nc.sbuf_base + 63) // 64) * 64
    AT = lambda n, s, off: nc.alloc_sbuf_tensor_at(n, s, F32, offset=S0 + off).ap()
    KB = 1024
    ATB1 = lambda n, s, off: nc.alloc_sbuf_tensor_at(
        n, s, BF16, offset=S0 + off).ap()
    w0h = ATB1("w0h", [128, HC, D], 0)
    w0l = ATB1("w0l", [128, HC, D], 16 * KB)
    w1h = ATB1("w1h", [128, HC, D], 32 * KB)
    w1l = ATB1("w1l", [128, HC, D], 48 * KB)
    wstage = AT("wstage", [128, 2, D], 64 * KB)
    ctx = AT("ctx", [128, HC, 1025], 72 * KB)
    hThs = [ATB1("hTh0", [128, D], 105 * KB), ATB1("hTh1", [128, D], 107 * KB)]
    hTls = [ATB1("hTl0", [128, D], 109 * KB), ATB1("hTl1", [128, D], 111 * KB)]
    ehs = [ATB1("eh0", [128, D], 113 * KB), ATB1("eh1", [128, D], 115 * KB)]
    els = [ATB1("el0", [128, D], 117 * KB), ATB1("el1", [128, D], 119 * KB)]
    vhs = [ATB1("vh0", [128, D], 121 * KB), ATB1("vh1", [128, D], 123 * KB)]
    vls = [ATB1("vl0", [128, D], 125 * KB), ATB1("vl1", [128, D], 127 * KB)]
    ef32 = AT("ef32", [128, D], 129 * KB)
    wslot_x = [AT("wkst%d" % i, [128, D], (113 + 4 * i) * KB)
               for i in range(3)]
    cWh = ATB1("cWh", [128, HC, D], 113 * KB)
    cWl = ATB1("cWl", [128, HC, D], 129 * KB)
    ctTh = ATB1("ctTh", [128, D], 145 * KB)
    ctTl = ATB1("ctTl", [128, D], 147 * KB)
    q_sbs = [AT("q_sb0", [128, D], 149 * KB), AT("q_sb1", [128, D], 153 * KB)]
    qThs = [ATB1("qTh0", [128, D], 157 * KB), ATB1("qTh1", [128, D], 159 * KB)]
    qTls = [ATB1("qTl0", [128, D], 161 * KB), ATB1("qTl1", [128, D], 163 * KB)]
    ATB = lambda n, s, off: nc.alloc_sbuf_tensor_at(
        n, s, BF16, offset=S0 + off).ap()
    nA = ATB("nA", [128, HC, T], 0)
    nB = ATB("nB", [128, HC, T], 32 * KB)
    wnat = AT("wnat", [128, 2, 2, D], 64 * KB)
    wTch = ATB("wTch", [128, 2, 2, HC * 128], 80 * KB)
    wTcl = ATB("wTcl", [128, 2, 2, HC * 128], 88 * KB)
    stage = AT("stage", [128, 4, T], 96 * KB)
    sq_scr2 = AT("sq_scr2", [128, D], 128 * KB)
    p2scr = AT("p2scr", [128, D], 132 * KB)
    gstage = AT("gstage", [128, 6, T], 0)
    wdTh = ATB("wdTh", [128, IC, D], 48 * KB)
    wdTl = ATB("wdTl", [128, IC, D], 96 * KB)
    w4nat = AT("w4nat", [128, 2, D], 144 * KB)
    w4T = AT("w4T", [128, 2, HC * 128], 152 * KB)
    ext0 = AT("ext0", [128, T], 144 * KB)
    ext1 = AT("ext1", [128, T], 152 * KB)
    oTs = [AT("oT0", [128, IC * 128], 0),
           AT("oT1", [128, IC * 128], 12 * KB)]
    scr3 = AT("scr3", [128, IC * 128], 24 * KB)
    a2os = [ATB("a2o0", [128, IC * 128], 36 * KB),
            ATB("a2o1", [128, IC * 128], 144 * KB)]
    b2os = [ATB("b2o0", [128, IC * 128], 42 * KB),
            ATB("b2o1", [128, IC * 128], 150 * KB)]
    out_sb = AT("out_sb", [128, D], 156 * KB)

    PA = lambda n, s: nc.alloc_psum_tensor(n, s, F32).ap()
    ptr8 = PA("ptr8", [128, D])
    pkw = PA("pkw", [128, D])
    pvw = PA("pvw", [128, D])
    pk0, pk1 = pkw[:, 0:512], pkw[:, 512:1024]
    pv0, pv1 = pvw[:, 0:512], pvw[:, 512:1024]
    pc = PA("pc", [128, 512])
    ps8 = PA("ps8", [128, 128])

    p = Prog(nc, es)

    MM = lambda out, l, r, st, sp: (lambda e: e.matmul(
        out, l, r, start=st, stop=sp, skip_group_check=True))
    TR = lambda out, in_: (lambda e: e.transpose(out, in_, ident))

    def transpose8(src, src_name, dst, dst_name):
        p.chain("pe", [TR(ptr8[:, k * 128:(k + 1) * 128],
                          src[:, k * 128:(k + 1) * 128]) for k in range(HC)],
                reads=[src_name, "ident"], writes=["ptr8"])
        p.op("dv", lambda e: e.tensor_copy(dst, ptr8),
             reads=["ptr8"], writes=[dst_name])

    def ln_stats(src_name, src, mo, io, mname, iname):
        p.op("dv", lambda e: e.bn_stats(ln6[:, 0:6], src[:, 0:512]),
             reads=[src_name], writes=["ln6"])
        p.op("dv", lambda e: e.bn_stats(ln6[:, 6:12], src[:, 512:1024]),
             reads=[src_name], writes=["ln6"])
        p.op("dv", lambda e: e.bn_aggr(lns2, ln6),
             reads=["ln6"], writes=["lns2"])
        p.op("ac", lambda e: e.activation(t128c, lns2[:, 1:2], AF.Ln,
                                          bias=epsc),
             reads=["lns2", "c_eps"], writes=["t_c"])
        p.op("ac", lambda e, io=io: e.activation(io, t128c, AF.Exp,
                                                 scale=-0.5),
             reads=["t_c"], writes=[iname])
        if mo is not None:
            p.op("dv", lambda e, mo=mo: e.tensor_copy(mo, lns2[:, 0:1]),
                 reads=["lns2"], writes=[mname])

    def exp_shift(wname, wbuf, dst, dstname):
        for kc in range(HC):
            for half in range(2):
                tgt = pk0 if half == 0 else pk1
                tn = "pk0" if half == 0 else "pk1"
                scr, scrn = ((sq_scr, "sq_scr") if half == 0
                             else (q_sbs[0][:, 0:512], "q_sb0"))
                p.op("ac", lambda e, kc=kc, half=half, scr=scr: e.activation(
                    scr, wbuf[:, kc, half * 512:(half + 1) * 512],
                    AF.Square),
                    reads=[wname], writes=[scrn])
                p.chain("pe", [MM(tgt[0:1, :], onesc, scr,
                                  kc == 0, kc == HC - 1)],
                        reads=[scrn, "c_onesc"], writes=[tn])
        p.op("dv", lambda e: e.reduce_max(s1x1[0:1, 0:1], pk0[0:1, :], AX.X),
             reads=["pk0"], writes=["s1x1"])
        p.op("dv", lambda e: e.reduce_max(s1x1[0:1, 1:2], pk1[0:1, :], AX.X),
             reads=["pk1"], writes=["s1x1"])
        p.op("dv", lambda e: e.tensor_max(s1x1[0:1, 2:3], s1x1[0:1, 0:1],
                                          s1x1[0:1, 1:2]),
             reads=["s1x1"], writes=["s1x1"])
        p.op("ac", lambda e: e.activation(s1x1[0:1, 3:4], s1x1[0:1, 2:3],
                                          AF.Ln, bias=epsc[0:1, :]),
             reads=["s1x1", "c_eps"], writes=["s1x1"])
        p.op("ac", lambda e: e.activation(s1x1[0:1, 4:5], s1x1[0:1, 3:4],
                                          AF.Exp, scale=0.5, bias=ln32),
             reads=["s1x1", "c_ln32"], writes=["s1x1"])
        p.chain("pe", [MM(ps8[:, 0:1], onesr, s1x1[0:1, 4:5], True, True)],
                reads=["s1x1", "c_onesr"], writes=["ps8"])
        p.op("dv", lambda e: e.tensor_scalar_mul(dst, ps8[:, 0:1], -1.0),
             reads=["ps8"], writes=[dstname])

    def round_act(buf, bufname, scale_ap, scale_read="scal"):
        p.op("ac", lambda e: e.activation(buf, buf, AF.Copy, scale=scale_ap,
                                          bias=MAGIC),
             reads=[bufname, scale_read], writes=[bufname])
        p.op("ac", lambda e: e.activation(buf, buf, AF.Copy, bias=-MAGIC),
             reads=[bufname], writes=[bufname])

    def round_rne(buf, bufname, scale_ap=None, scale_read=None):
        reads = [bufname] + ([scale_read] if scale_read else [])
        if scale_ap is not None:
            p.op("dv", lambda e: e.tensor_scalar(buf, buf, scale_ap, MAGIC,
                                                 ALU.mult, ALU.add),
                 reads=reads, writes=[bufname])
        else:
            p.op("dv", lambda e: e.tensor_scalar(buf, buf, MAGIC, None,
                                                 ALU.add, ALU.bypass),
                 reads=reads, writes=[bufname])
        p.op("dv", lambda e: e.tensor_scalar(buf, buf, MAGIC, None,
                                             ALU.subtract, ALU.bypass),
             reads=[bufname], writes=[bufname])

    p.op("dv", lambda e: e.memset(epsc, EPS), writes=["c_eps"])
    p.op("dv", lambda e: e.memset(ln32, float(np.log(np.float32(32.0 * 1.002)))),
         writes=["c_ln32"])
    p.op("dv", lambda e: e.memset(ctx, 0.0), writes=["ctx"])
    p.op("dv", lambda e: e.memset(gmax, 0.0), writes=["gmax"])
    p.op("dv", lambda e: e.memset(onesc_bf, 1.0), writes=["c_onesc_bf"])
    p.dma("qc", "sp", ident, ident_h, writes=["ident"])
    p.dma("qc", "sp", onesc, onesc_h, writes=["c_onesc"])
    p.dma("qc", "sp", onesr, onesr_h, writes=["c_onesr"])
    def load_w_pair(hbm, hbuf, lbuf, base):
        wslots = [(wstage[:, 0, :], "wstage0", []),
                  (wstage[:, 1, :], "wstage1", []),
                  (ef32, "ef32", []),
                  (wslot_x[0], "wkst0", ["e0h", "e1h"]),
                  (wslot_x[1], "wkst1", ["e0l", "e1l"]),
                  (wslot_x[2], "wkst2", ["v0h", "v1h"])]
        ns = len(wslots)
        for c in range(HC):
            ws, wsn, cov = wslots[c % ns]
            p.dma("qws%d" % (c % ns), "sp", ws,
                  hbm[c * 128:(c + 1) * 128, :], writes=[wsn] + cov)
            p.op("ac", lambda e, c=c, ws=ws: e.activation(
                hbuf[:, c, :], ws, AF.Copy),
                reads=[wsn], writes=[base + "h"])
            p.op("dv", lambda e, c=c, ws=ws: e.tensor_tensor(
                lbuf[:, c, :], ws, hbuf[:, c, :], ALU.subtract),
                reads=[wsn, base + "h"], writes=[base + "l"])

    p.dma("qc", "sp", bo_bc[0:1, :], bo_h, writes=["bo_bc"])
    for t0 in range(min(3, nT)):
        p.dma("qxt%d" % t0, "sp", xts[t0],
              x_in.rearrange("(t p) f -> p t f", p=128)[:, t0, :],
              writes=["xt%d" % t0])
    for half in range(2):
        tgt, tn = (pk0, "pk0") if half == 0 else (pk1, "pk1")
        p.chain("pe", [MM(tgt, onesr, bo_bc[0:1, half * 512:(half + 1) * 512],
                          True, True)],
                reads=["bo_bc", "c_onesr"], writes=[tn])
    p.op("dv", lambda e: e.tensor_copy(bo_bc[:, 0:512], pk0),
         reads=["pk0"], writes=["bo_bc"])
    p.op("dv", lambda e: e.tensor_copy(bo_bc[:, 512:1024], pk1),
         reads=["pk1"], writes=["bo_bc"])

    cbanks = [(pc, "pc"), (pv0, "pv0"), (pv1, "pv1"), (pk0, "pk0"),
              (pk1, "pk1")]

    def front_dma(t):
        px = t % 4
        p.dma("qxt%d" % px, "sp", xts[px],
              x_in.rearrange("(t p) f -> p t f", p=128)[:, t, :],
              writes=["xt%d" % px])

    def front_stats(t):
        par, px = t % 2, t % 4
        xn, hn = "xt%d" % px, "h%d" % par
        ln_stats(xn, xts[px], None, t128b, None, "t_inv")
        p.op("dv", lambda e, par=par, px=px: e.tensor_scalar(
            hs[par], xts[px], lns2[:, 0:1], t128b, ALU.subtract, ALU.mult),
            reads=[xn, "lns2", "t_inv"], writes=[hn])

    def front_tr(t):
        par = t % 2
        hn, hTn = "h%d" % par, "hT%d" % par
        p.chain("pe", [TR(ptr8[:, k * 128:(k + 1) * 128],
                          hs[par][:, k * 128:(k + 1) * 128])
                       for k in range(HC)],
                reads=[hn, "ident"], writes=["ptr8"])
        p.op("ac", lambda e, par=par: e.activation(hThs[par], ptr8, AF.Copy),
             reads=["ptr8"], writes=[hTn + "h"])
        p.op("dv", lambda e, par=par: e.tensor_tensor(
            hTls[par], ptr8, hThs[par], ALU.subtract),
            reads=["ptr8", hTn + "h"], writes=[hTn + "l"])

    def front(t):
        front_dma(t)
        front_stats(t)
        front_tr(t)

    def proj3(tgt, par, wh, wl, whn, wln, half):
        fns = []
        for kc in range(HC):
            sl = slice(kc * 128, (kc + 1) * 128)
            ws = (slice(None), kc, slice(half * 512, (half + 1) * 512))
            fns.append(MM(tgt, hThs[par][:, sl], wh[ws], kc == 0, False))
            fns.append(MM(tgt, hTls[par][:, sl], wh[ws], False, False))
            fns.append(MM(tgt, hThs[par][:, sl], wl[ws], False,
                          kc == HC - 1))
        return fns

    def backA(t):
        par = t % 2
        hTn = "hT%d" % par
        en, vn = "e%d" % par, "v%d" % par
        for half in range(2):
            tgt, tn = (pk0, "pk0") if half == 0 else (pk1, "pk1")
            p.chain("pe", proj3(tgt, par, w0h, w0l, "w0h", "w0l", half),
                    reads=[hTn + "h", hTn + "l", "w0h", "w0l"], writes=[tn])
        p.op("ac", lambda e: e.activation(ef32, pkw, AF.Exp, bias=negc),
             reads=["pk0", "pk1", "negc"], writes=["ef32"])
        p.op("ac", lambda e, par=par: e.activation(ehs[par], ef32, AF.Copy),
             reads=["ef32"], writes=[en + "h"])
        p.op("dv", lambda e, par=par: e.tensor_tensor(
            els[par], ef32, ehs[par], ALU.subtract),
            reads=["ef32", en + "h"], writes=[en + "l"])
        for half in range(2):
            tgt, tn = (pv0, "pv0") if half == 0 else (pv1, "pv1")
            p.chain("pe", proj3(tgt, par, w1h, w1l, "w1h", "w1l", half),
                    reads=[hTn + "h", hTn + "l", "w1h", "w1l"], writes=[tn])
        p.op("ac", lambda e, par=par: e.activation(vhs[par], pvw, AF.Copy),
             reads=["pv0", "pv1"], writes=[vn + "h"])
        p.op("dv", lambda e, par=par: e.tensor_tensor(
            vls[par], pvw, vhs[par], ALU.subtract),
            reads=["pv0", "pv1", vn + "h"], writes=[vn + "l"])

    def backA_ctx(t):
        par = t % 2
        en, vn = "e%d" % par, "v%d" % par
        for i in range(16):
            dc, half = i // 2, i % 2
            bank, bn = cbanks[i % len(cbanks)]
            dsl = slice(dc * 128, (dc + 1) * 128)
            hsl = slice(half * 512, (half + 1) * 512)
            p.chain("pe", [
                MM(bank, ehs[par][:, dsl], vhs[par][:, hsl], True, False),
                MM(bank, els[par][:, dsl], vhs[par][:, hsl], False, False),
                MM(bank, ehs[par][:, dsl], vls[par][:, hsl], False, True)],
                reads=[en + "h", en + "l", vn + "h", vn + "l"], writes=[bn])
            p.op("dv", lambda e, dc=dc, half=half, bank=bank: e.tensor_tensor(
                ctx[:, dc, half * 512:(half + 1) * 512],
                ctx[:, dc, half * 512:(half + 1) * 512], bank, ALU.add),
                reads=[bn, "ctx"], writes=["ctx"])
        fns = []
        for dc in range(HC):
            dsl = slice(dc * 128, (dc + 1) * 128)
            fns.append(MM(ps8[:, dc:dc + 1], ehs[par][:, dsl], onesc_bf,
                          True, False))
            fns.append(MM(ps8[:, dc:dc + 1], els[par][:, dsl], onesc_bf,
                          False, True))
        p.chain("pe", fns, reads=[en + "h", en + "l", "c_onesc_bf"],
                writes=["ps8"])
        p.op("dv", lambda e: e.tensor_tensor(
            ctx[:, :, 1024:1025], ctx[:, :, 1024:1025],
            ps8[:, 0:HC].unsqueeze(2), ALU.add),
            reads=["ps8", "ctx"], writes=["ctx"])

    front_stats(0)
    front_tr(0)
    if nT > 1:
        front_stats(1)
        front_tr(1)
    load_w_pair(wk_h, w0h, w0l, "w0")
    exp_shift("w0h", w0h, negc, "negc")
    load_w_pair(wv_h, w1h, w1l, "w1")
    for t in range(nT):
        if t + 2 < nT:
            front_stats(t + 2)
        backA(t)
        if t + 2 < nT:
            front_tr(t + 2)
        backA_ctx(t)
        if t + 3 < nT:
            front_dma(t + 3)

    p.fence()

    p.dma("qcc", "sp", cc_ctx_i, ctx, reads=["ctx"], writes=["cc_ctx_i"])
    p.op("gp", lambda e: e.collective_compute(
        "AllReduce", ALU.add, replica_groups=rg, ins=[cc_ctx_i],
        outs=[cc_ctx_o]), reads=["cc_ctx_i"], writes=["cc_ctx_o"])
    for t0b in range(min(3, nT)):
        front(t0b)
    load_w_pair(wq_h, w0h, w0l, "w0")
    exp_shift("w0h", w0h, negcq, "negcq")

    def emit_ctxwo():
        for dc in range(HC):
            p.chain("pe", [TR(ptr8[:, ec * 128:(ec + 1) * 128],
                              ctx[:, dc, ec * 128:(ec + 1) * 128])
                           for ec in range(HC)],
                    reads=["ctx", "ident"], writes=["ptr8"])
            p.op("ac", lambda e: e.activation(ctTh, ptr8, AF.Copy),
                 reads=["ptr8"], writes=["ctTh"])
            p.op("dv", lambda e: e.tensor_tensor(ctTl, ptr8, ctTh, ALU.subtract),
                 reads=["ptr8", "ctTh"], writes=["ctTl"])
            for half in range(2):
                tgt, tn = (pk0, "pk0") if half == 0 else (pk1, "pk1")
                fns = []
                for ec in range(HC):
                    sl = slice(ec * 128, (ec + 1) * 128)
                    ws = (slice(None), ec, slice(half * 512, (half + 1) * 512))
                    fns.append(MM(tgt, ctTh[:, sl], w1h[ws], ec == 0, False))
                    fns.append(MM(tgt, ctTl[:, sl], w1h[ws], False, False))
                    fns.append(MM(tgt, ctTh[:, sl], w1l[ws], False,
                                  ec == HC - 1))
                p.chain("pe", fns, reads=["ctTh", "ctTl", "w1h", "w1l"],
                        writes=[tn])
            p.op("ac", lambda e, dc=dc: e.activation(
                cWh[:, dc, :], pkw, AF.Copy, scale=rs8[:, dc:dc + 1]),
                reads=["pk0", "pk1", "rs8"], writes=["cWh"])
            p.op("dv", lambda e, dc=dc: e.scalar_tensor_tensor(
                cWl[:, dc, :], pkw, rs8[:, dc:dc + 1], cWh[:, dc, :],
                ALU.mult, ALU.subtract),
                reads=["pk0", "pk1", "rs8", "cWh"], writes=["cWl"])

    p.op("dv", lambda e: e.memset(gmax, 0.0), writes=["gmax"])

    def sq_mm(t):
        par = t % 2
        hTn, qn = "hT%d" % par, "q_sb%d" % par
        for half in range(2):
            tgt, tn = (pk0, "pk0") if half == 0 else (pk1, "pk1")
            p.chain("pe", proj3(tgt, par, w0h, w0l, "w0h", "w0l", half),
                    reads=[hTn + "h", hTn + "l", "w0h", "w0l"], writes=[tn])
        p.op("ac", lambda e, par=par: e.activation(
            q_sbs[par], pkw, AF.Exp, bias=negcq, accum_out=v128a),
            reads=["pk0", "pk1", "negcq"], writes=[qn, "v_a"])
        p.op("dv", lambda e: e.reciprocal(v128a, v128a),
             reads=["v_a"], writes=["v_a"])
        p.op("dv", lambda e, par=par: e.tensor_scalar_mul(
            q_sbs[par], q_sbs[par], v128a),
            reads=[qn, "v_a"], writes=[qn])

    def sq_tr(t):
        par = t % 2
        qn, qTn = "q_sb%d" % par, "qT%d" % par
        p.chain("pe", [TR(ptr8[:, k * 128:(k + 1) * 128],
                          q_sbs[par][:, k * 128:(k + 1) * 128])
                       for k in range(HC)],
                reads=[qn, "ident"], writes=["ptr8"])
        p.op("ac", lambda e, par=par: e.activation(qThs[par], ptr8, AF.Copy),
             reads=["ptr8"], writes=[qTn + "h"])
        p.op("dv", lambda e, par=par: e.tensor_tensor(
            qTls[par], ptr8, qThs[par], ALU.subtract),
            reads=["ptr8", qTn + "h"], writes=[qTn + "l"])

    def restB(t):
        par = t % 2
        qTn, x2n = "qT%d" % par, "x2t%d" % par
        for half in range(2):
            tgt, tn = (pv0, "pv0") if half == 0 else (pv1, "pv1")
            fns = []
            for dc in range(HC):
                sl = slice(dc * 128, (dc + 1) * 128)
                ws = (slice(None), dc, slice(half * 512, (half + 1) * 512))
                fns.append(MM(tgt, qThs[par][:, sl], cWh[ws], dc == 0, False))
                fns.append(MM(tgt, qTls[par][:, sl], cWh[ws], False, False))
                fns.append(MM(tgt, qThs[par][:, sl], cWl[ws], False,
                              dc == HC - 1))
            p.chain("pe", fns,
                    reads=[qTn + "h", qTn + "l", "cWh", "cWl"], writes=[tn])

    def restB_add(t):
        par = t % 2
        x2n = "x2t%d" % par
        p.op("dv", lambda e, par=par: e.tensor_tensor(
            x2ts[par], pvw, bo_bc, ALU.add),
            reads=["pv0", "pv1", "bo_bc"], writes=[x2n])
        p.op("dv", lambda e, par=par, px=t % 4: e.tensor_add(
            x2ts[par], x2ts[par], xts[px]),
            reads=[x2n, "xt%d" % (t % 4)], writes=[x2n])
        p.dma("qsx2", "gp", sp_x2[:, t, :], x2ts[par], reads=[x2n],
              writes=["sp_x2"])

    def restB_stats(t):
        par = t % 2
        x2n = "x2t%d" % par
        ln_stats(x2n, x2ts[par], stats[:, 2 * t:2 * t + 1],
                 stats[:, 2 * t + 1:2 * t + 2], "stats", "stats")
        p.op("dv", lambda e: e.tensor_copy(u128a, lns2[:, 0:1]),
             reads=["lns2"], writes=["u_mean"])
        p.op("dv", lambda e, par=par: e.reduce_max(t128c, x2ts[par], AX.X),
             reads=[x2n], writes=["t_c"])
        p.op("dv", lambda e, par=par: e.tensor_reduce(t128d, x2ts[par], AX.X,
                                                      ALU.min),
             reads=[x2n], writes=["t_d"])
        p.op("dv", lambda e: e.tensor_scalar(t128c, t128c, u128a, None,
                                             ALU.subtract, ALU.bypass),
             reads=["t_c", "u_mean"], writes=["t_c"])
        p.op("dv", lambda e: e.tensor_scalar(t128d, t128d, u128a, -1.0,
                                             ALU.subtract, ALU.mult),
             reads=["t_d", "u_mean"], writes=["t_d"])
        p.op("dv", lambda e: e.tensor_max(t128c, t128c, t128d),
             reads=["t_c", "t_d"], writes=["t_c"])
        p.op("dv", lambda e, t=t: e.tensor_mul(
            t128c, t128c, stats[:, 2 * t + 1:2 * t + 2]),
            reads=["t_c", "stats"], writes=["t_c"])
        p.op("dv", lambda e: e.tensor_max(gmax[:, 0:1], gmax[:, 0:1], t128c),
             reads=["t_c", "gmax"], writes=["gmax"])

    sq_mm(0)
    if nT > 1:
        sq_mm(1)
    sq_tr(0)
    if nT > 1:
        sq_tr(1)
    load_w_pair(wo_h, w1h, w1l, "w1")
    p.dma("qcc", "sp", ctx, cc_ctx_o, reads=["cc_ctx_o"], writes=["ctx"])
    for dc in range(HC):
        p.op("dv", lambda e, dc=dc: e.reciprocal(t128a, ctx[:, dc, 1024:1025]),
             reads=["ctx"], writes=["t_mean"])
        p.op("dv", lambda e, dc=dc: e.tensor_scalar_mul(
            rs8[:, dc:dc + 1], t128a, 0.125 / float(D)),
            reads=["t_mean"], writes=["rs8"])
    emit_ctxwo()
    if nT > 3:
        front_dma(3)
    for t in range(nT):
        if t + 3 < nT:
            front_stats(t + 3)
        restB(t)
        if t + 3 < nT:
            front_tr(t + 3)
        if t + 1 < nT and t > 0:
            sq_tr(t + 1)
        restB_add(t)
        if t + 2 < nT:
            sq_mm(t + 2)
        restB_stats(t)
        if t + 4 < nT:
            front_dma(t + 4)

    def scalar_allreduce(cols, k, nvals, pre_emit=None):
        for j, col in enumerate(cols):
            p.chain("pe", [TR(ptr8[0:1, 0:128], gmax[:, col:col + 1])],
                    reads=["gmax", "ident"], writes=["ptr8"])
            p.op("dv", lambda e: e.tensor_copy(srow, ptr8[0:1, 0:128]),
                 reads=["ptr8"], writes=["srow"])
            p.op("dv", lambda e, j=j: e.reduce_max(s1x1[0:1, j:j + 1], srow,
                                                   AX.X),
                 reads=["srow"], writes=["s1x1"])
        p.dma("qcc", "sp", cc_i[k][0:1, 0:nvals], s1x1[0:1, 0:nvals],
              reads=["s1x1"], writes=["cc_i%d" % k])
        p.op("gp", lambda e, k=k: e.collective_compute(
            "AllReduce", ALU.max, replica_groups=rg,
            ins=[cc_i[k][0:1, 0:nvals]], outs=[cc_o[k][0:1, 0:nvals]]),
            reads=["cc_i%d" % k], writes=["cc_o%d" % k])
        if pre_emit is not None:
            pre_emit()
        p.dma("qcc", "sp", s1x1[0:1, 0:nvals], cc_o[k][0:1, 0:nvals],
              reads=["cc_o%d" % k], writes=["s1x1"])

    def bcast_cols(c0, n):
        p.chain("pe", [MM(ps8[:, 0:n], onesr, m1x1[0:1, c0:c0 + n],
                          True, True)],
                reads=["m1x1", "c_onesr"], writes=["ps8"])
        p.op("dv", lambda e, c0=c0, n=n: e.tensor_copy(
            scal[:, c0:c0 + n], ps8[:, 0:n]),
            reads=["ps8"], writes=["scal"])

    def bcast_scale(src_col, rcol, scol, pre=None):
        if pre is not None:
            p.op("dv", lambda e: e.tensor_mul(
                s1x1[0:1, src_col:src_col + 1],
                s1x1[0:1, src_col:src_col + 1], m1x1[0:1, pre:pre + 1]),
                reads=["s1x1", "m1x1"], writes=["s1x1"])
        p.op("dv", lambda e: e.tensor_scalar(
            m1x1[0:1, scol:scol + 1], s1x1[0:1, src_col:src_col + 1],
            1.0 / QMAX, 1e-12, ALU.mult, ALU.max),
            reads=["s1x1"], writes=["m1x1"])
        p.op("dv", lambda e: e.reciprocal(m1x1[0:1, rcol:rcol + 1],
                                          m1x1[0:1, scol:scol + 1]),
             reads=["m1x1"], writes=["m1x1"])
        bcast_cols(min(rcol, scol), abs(scol - rcol) + 1)

    def fold_mul(dst, a, b):
        p.op("dv", lambda e: e.tensor_mul(
            m1x1[0:1, dst:dst + 1], m1x1[0:1, a:a + 1], m1x1[0:1, b:b + 1]),
            reads=["m1x1"], writes=["m1x1"])
        bcast_cols(dst, 1)

    def p2p_load(t):
        par = t % 2
        p.dma("qx2%d" % par, "sp", x2ts[par], sp_x2[:, t, :],
              reads=["sp_x2"], writes=["x2t%d" % par])
        p.op("dv", lambda e, t=t, par=par: e.tensor_scalar(
            hs[par], x2ts[par], stats[:, 2 * t:2 * t + 1],
            stats[:, 2 * t + 1:2 * t + 2], ALU.subtract, ALU.mult),
            reads=["x2t%d" % par, "stats"], writes=["h%d" % par])

    def p2p_main(t):
        par = t % 2
        hn = "h%d" % par
        round_act(hs[par], hn, scal[:, 0:1])
        p.chain("pe", [TR(ptr8[:, k * 128:(k + 1) * 128],
                          hs[par][:, k * 128:(k + 1) * 128])
                       for k in range(HC)],
                reads=[hn, "ident"], writes=["ptr8"])
        p.op("ac", lambda e: e.activation(p2scr, ptr8, AF.Copy,
                                          scale=1.0 / 256.0, bias=MAGIC),
             reads=["ptr8"], writes=["p2scr"])
        p.op("dv", lambda e, t=t: e.tensor_scalar(
            nA[:, :, t * 128:(t + 1) * 128],
            p2scr.rearrange("p (c k) -> p c k", k=128), MAGIC, 256.0,
            ALU.subtract, ALU.mult),
            reads=["p2scr"], writes=["nA"])
        p.op("dv", lambda e, t=t: e.scalar_tensor_tensor(
            nB[:, :, t * 128:(t + 1) * 128],
            nA[:, :, t * 128:(t + 1) * 128], -1.0,
            ptr8.rearrange("p (c k) -> p c k", k=128), ALU.mult, ALU.add),
            reads=["nA", "ptr8"], writes=["nB"])

    scalar_allreduce([0], 0, 1,
                     pre_emit=lambda: (p2p_load(0), p2p_load(1)))
    bcast_scale(0, 0, 1)

    p.fence()

    for t in range(nT):
        p2p_main(t)
        if t + 2 < nT:
            p2p_load(t + 2)

    def dequant_dve(nat, natname, scr, scrname, eng="dv", eng_rnd=None):
        if eng_rnd is None:
            eng_rnd = eng
        nb = nat.shape[-1] // 32
        bv = lambda a: a.rearrange("p (b k) -> p b k", k=32)
        p.op("dv", lambda e: e.tensor_reduce(
            blk[:, 0:nb], bv(nat), AX.X, ALU.max, apply_absolute_value=True),
            reads=[natname], writes=["blk"])
        p.op("dv", lambda e: e.tensor_scalar_mul(blk2[:, 0:nb], blk[:, 0:nb],
                                                 1.0 / WQMAX),
             reads=["blk"], writes=["blk2"])
        p.op("dv", lambda e: e.scalar_tensor_tensor(
            blk3[:, 0:nb], blk2[:, 0:nb], -WQMAX, blk[:, 0:nb],
            ALU.mult, ALU.add),
            reads=["blk", "blk2"], writes=["blk3"])
        p.op("dv", lambda e: e.scalar_tensor_tensor(
            blk2[:, 0:nb], blk3[:, 0:nb], 1.0 / WQMAX, blk2[:, 0:nb],
            ALU.mult, ALU.add),
            reads=["blk2", "blk3"], writes=["blk2"])
        p.op("dv", lambda e: e.tensor_scalar(blk2[:, 0:nb], blk2[:, 0:nb],
                                             1e-12, None, ALU.max, ALU.bypass),
             reads=["blk2"], writes=["blk2"])
        p.op("dv", lambda e: e.reciprocal(blk3[:, 0:nb], blk2[:, 0:nb]),
             reads=["blk2"], writes=["blk3"])
        p.op("dv", lambda e: e.tensor_mul(blk[:, 0:nb], blk3[:, 0:nb],
                                          blk2[:, 0:nb]),
             reads=["blk2", "blk3"], writes=["blk"])
        p.op("dv", lambda e: e.tensor_scalar(blk[:, 0:nb], blk[:, 0:nb],
                                             2.0, -1.0, ALU.subtract,
                                             ALU.mult),
             reads=["blk"], writes=["blk"])
        p.op("dv", lambda e: e.tensor_mul(blk3[:, 0:nb], blk3[:, 0:nb],
                                          blk[:, 0:nb]),
             reads=["blk", "blk3"], writes=["blk3"])
        p.op(eng, lambda e: e.tensor_tensor(
            bv(scr), bv(nat), blk3[:, 0:nb].unsqueeze(2)
            .broadcast_to([128, nb, 32]), ALU.mult),
            reads=[natname, "blk3"], writes=[scrname])
        p.op(eng_rnd, lambda e: e.tensor_scalar(scr, scr, MAGIC, MAGIC,
                                                ALU.add, ALU.subtract),
             reads=[scrname], writes=[scrname])
        p.op(eng, lambda e: e.tensor_tensor(
            bv(scr), bv(scr), blk2[:, 0:nb].unsqueeze(2)
            .broadcast_to([128, nb, 32]), ALU.mult),
            reads=[scrname, "blk2"], writes=[scrname])

    def dequant_tr(scr, scrname, wT_hi, wT_lo, wT_name):
        p.chain("pe", [TR(ptr8[:, k * 128:(k + 1) * 128],
                          scr[:, k * 128:(k + 1) * 128]) for k in range(HC)],
                reads=[scrname, "ident"], writes=["ptr8"])
        p.op("ac", lambda e: e.activation(wT_hi, ptr8, AF.Copy),
             reads=["ptr8"], writes=[wT_name + "h"])
        p.op("dv", lambda e: e.tensor_tensor(wT_lo, ptr8, wT_hi,
                                             ALU.subtract),
             reads=["ptr8", wT_name + "h"], writes=[wT_name + "l"])

    P2SPILL = [(sp_up, "sp_up", 1), (sp_gate, "sp_gate", 2)]
    p2banks = [(pk0, "pk0"), (pk1, "pk1"), (pv0, "pv0"), (pv1, "pv1")]
    p2cnt = [0]

    def p2_loads(jc):
        par = jc % 2
        p.dma("qwn%d0" % par, "sp", wnat[:, par, 0, :],
              wu_h[jc * 128:(jc + 1) * 128, :], writes=["wnat%d0" % par])
        p.dma("qwn%d1" % par, "sp", wnat[:, par, 1, :],
              wg_h[jc * 128:(jc + 1) * 128, :], writes=["wnat%d1" % par])

    def p2_dequant_dve(jc):
        par = jc % 2
        for wi in range(2):
            scr, scrn = (p2scr, "p2scr") if wi == 0 else (sq_scr2, "sq_scr2")
            dequant_dve(wnat[:, par, wi, :], "wnat%d%d" % (par, wi),
                        scr, scrn)

    def p2_dequant_tr(jc):
        par = jc % 2
        dequant_tr(p2scr, "p2scr", wTch[:, par, 0, :], wTcl[:, par, 0, :],
                   "wT%d0" % par)
        w1h_, w1l_ = wTch[:, par, 1, :], wTcl[:, par, 1, :]
        wn = "wT%d1" % par
        p.chain("pe", [TR(pc[:, k * 128:(k + 1) * 128],
                          sq_scr2[:, k * 128:(k + 1) * 128])
                       for k in range(4)],
                reads=["sq_scr2", "ident"], writes=["pc"])
        p.op("ac", lambda e, w1h_=w1h_: e.activation(
            w1h_[:, 0:512], pc, AF.Copy), reads=["pc"], writes=[wn + "h"])
        p.op("dv", lambda e, w1h_=w1h_, w1l_=w1l_: e.tensor_tensor(
            w1l_[:, 0:512], pc, w1h_[:, 0:512], ALU.subtract),
            reads=["pc", wn + "h"], writes=[wn + "l"])
        p.chain("pe", [TR(ptr8[:, k * 128:(k + 1) * 128],
                          sq_scr2[:, (4 + k) * 128:(5 + k) * 128])
                       for k in range(4)],
                reads=["sq_scr2", "ident"], writes=["ptr8"])
        p.op("ac", lambda e, w1h_=w1h_: e.activation(
            w1h_[:, 512:1024], ptr8[:, 0:512], AF.Copy),
            reads=["ptr8"], writes=[wn + "h"])
        p.op("dv", lambda e, w1h_=w1h_, w1l_=w1l_: e.tensor_tensor(
            w1l_[:, 512:1024], ptr8[:, 0:512], w1h_[:, 512:1024],
            ALU.subtract),
            reads=["ptr8", wn + "h"], writes=[wn + "l"])

    def p2_mms(jc):
        par = jc % 2
        sl2 = (jc % 2) * 2
        for wi, (spill, spname, gcol) in enumerate(P2SPILL):
            wtn = "wT%d%d" % (par, wi)
            stn = "stage%d" % (sl2 + wi)
            for st in range(nST):
                tgt, tn = p2banks[p2cnt[0] % 4]
                p2cnt[0] += 1
                fns = []
                for kc in range(HC):
                    for pi, (wt, na) in enumerate(
                            [(wTch, nA), (wTcl, nA), (wTch, nB)]):
                        fns.append(MM(
                            tgt[:, 0:STW],
                            wt[:, par, wi, kc * 128:(kc + 1) * 128],
                            na[:, kc, st * STW:(st + 1) * STW],
                            kc == 0 and pi == 0,
                            kc == HC - 1 and pi == 2))
                p.chain("pe", fns,
                        reads=[wtn + "h", wtn + "l", "nA", "nB"], writes=[tn])
                p.op("ac", lambda e, tgt=tgt, wi=wi, st=st, sl2=sl2:
                     e.activation(stage[:, sl2 + wi, st * STW:(st + 1) * STW],
                                  tgt[:, 0:STW], AF.Copy, scale=scal[:, 1:2]),
                     reads=[tn, "scal"], writes=[stn])
            p.op("dv", lambda e, wi=wi, sl2=sl2: e.tensor_reduce(
                t128e, stage[:, sl2 + wi, :], AX.X, ALU.max,
                apply_absolute_value=True),
                reads=[stn], writes=["t_e"])
            p.op("dv", lambda e, gcol=gcol: e.tensor_max(
                gmax[:, gcol:gcol + 1], gmax[:, gcol:gcol + 1], t128e),
                reads=["t_e", "gmax"], writes=["gmax"])
            p.dma("qsp%d" % wi, "gp", spill[jc, :, :], stage[:, sl2 + wi, :],
                  reads=[stn], writes=[spname])

    NWU = HC * (I3 // D)

    def wd_dma(u):
        hcp, piece, par = u // 3, u % 3, u % 2
        p.dma("qwd%d" % par, "sp", w4nat[:, par, :],
              wd_h[hcp * 128:(hcp + 1) * 128, piece * D:(piece + 1) * D],
              writes=["w4nat%d" % par])

    def wd_dq1(u):
        par = u % 2
        nat, natname = w4nat[:, par, :], "w4nat%d" % par
        scr, scrname = w4T[:, par, :], "w4T%d" % par
        bA, bB, bC = wblkA[par], wblkB[par], wblkC[par]
        An, Bn, Cn = "wblkA%d" % par, "wblkB%d" % par, "wblkC%d" % par
        bv = lambda a: a.rearrange("p (b k) -> p b k", k=32)
        p.op("dv", lambda e, nat=nat, bC=bC: e.tensor_reduce(
            bC, bv(nat), AX.X, ALU.max, apply_absolute_value=True),
            reads=[natname], writes=[Cn])
        p.op("dv", lambda e, bA=bA, bC=bC: e.tensor_scalar_mul(
            bA, bC, 1.0 / WQMAX), reads=[Cn], writes=[An])
        p.op("dv", lambda e, bA=bA, bB=bB, bC=bC: e.scalar_tensor_tensor(
            bB, bA, -WQMAX, bC, ALU.mult, ALU.add),
            reads=[An, Cn], writes=[Bn])
        p.op("dv", lambda e, bA=bA, bB=bB: e.scalar_tensor_tensor(
            bA, bB, 1.0 / WQMAX, bA, ALU.mult, ALU.add),
            reads=[An, Bn], writes=[An])
        p.op("dv", lambda e, bA=bA: e.tensor_scalar(
            bA, bA, 1e-12, None, ALU.max, ALU.bypass),
            reads=[An], writes=[An])
        p.op("dv", lambda e, bA=bA, bB=bB: e.reciprocal(bB, bA),
             reads=[An], writes=[Bn])
        p.op("dv", lambda e, bA=bA, bB=bB, bC=bC: e.tensor_mul(bC, bB, bA),
             reads=[An, Bn], writes=[Cn])
        p.op("dv", lambda e, bC=bC: e.tensor_scalar(
            bC, bC, 2.0, -1.0, ALU.subtract, ALU.mult),
            reads=[Cn], writes=[Cn])
        p.op("dv", lambda e, bB=bB, bC=bC: e.tensor_mul(bB, bB, bC),
             reads=[Bn, Cn], writes=[Bn])
        p.op("gp", lambda e, nat=nat, scr=scr, bB=bB: e.tensor_tensor(
            bv(scr), bv(nat), bB.unsqueeze(2).broadcast_to([128, 32, 32]),
            ALU.mult), reads=[natname, Bn], writes=[scrname])

    def wd_dq2(u):
        par = u % 2
        scr, scrname = w4T[:, par, :], "w4T%d" % par
        bA, An = wblkA[par], "wblkA%d" % par
        bv = lambda a: a.rearrange("p (b k) -> p b k", k=32)
        p.op("dv", lambda e, scr=scr: e.tensor_scalar(
            scr, scr, MAGIC, MAGIC, ALU.add, ALU.subtract),
            reads=[scrname], writes=[scrname])
        p.op("gp", lambda e, scr=scr, bA=bA: e.tensor_tensor(
            bv(scr), bv(scr), bA.unsqueeze(2).broadcast_to([128, 32, 32]),
            ALU.mult), reads=[scrname, An], writes=[scrname])

    def wd_spill(u):
        par = u % 2
        p.dma("qwq%d" % par, "sp", sp_wdq[u, :, :], w4T[:, par, :],
              reads=["w4T%d" % par], writes=["sp_wdq"])

    p2_loads(0)
    p2_loads(1)
    p2_dequant_dve(0)
    p2_dequant_tr(0)
    wd_dma(0)
    for jc in range(IC):
        if jc + 2 < IC:
            p2_loads(jc + 2)
        if jc + 1 < IC:
            p2_dequant_dve(jc + 1)
        p2_mms(jc)
        if jc + 1 < IC:
            p2_dequant_tr(jc + 1)
        if jc + 1 < NWU:
            wd_dma(jc + 1)
        wd_dq1(jc)
        wd_dq2(jc)
        wd_spill(jc)

    def wd_ld(u):
        par = u % 2
        p.dma("qwl%d" % par, "gp", w4T[:, par, :], sp_wdq[u, :, :],
              reads=["sp_wdq"], writes=["w4T%d" % par])

    def wd_tr(u):
        hcp, piece, par = u // 3, u % 3, u % 2
        p.chain("pe", [TR(ptr8[:, k * 128:(k + 1) * 128],
                          w4T[:, par, k * 128:(k + 1) * 128])
                       for k in range(HC)],
                reads=["w4T%d" % par, "ident"], writes=["ptr8"])
        dsth = wdTh[:, piece * HC:(piece + 1) * HC,
                    hcp * 128:(hcp + 1) * 128]
        dstl = wdTl[:, piece * HC:(piece + 1) * HC,
                    hcp * 128:(hcp + 1) * 128]
        pview = ptr8.rearrange("p (c k) -> p c k", k=128)
        p.op("ac", lambda e, dsth=dsth, pview=pview: e.activation(
            dsth, pview, AF.Copy),
            reads=["ptr8"], writes=["wdTh"])
        p.op("dv", lambda e, dsth=dsth, dstl=dstl, pview=pview:
             e.tensor_tensor(dstl, pview, dsth, ALU.subtract),
             reads=["ptr8", "wdTh"], writes=["wdTl"])

    def round_split(buf, bufname, scale_ap):
        p.op("ac", lambda e, buf=buf: e.activation(buf, buf, AF.Copy,
                                                   scale=scale_ap, bias=MAGIC),
             reads=[bufname, "scal"], writes=[bufname])
        p.op("dv", lambda e, buf=buf: e.tensor_scalar(buf, buf, MAGIC, None,
                                                      ALU.subtract,
                                                      ALU.bypass),
             reads=[bufname], writes=[bufname])

    p3_g = [(gstage[:, i, 0:T], "gst%d" % i) for i in range(3)]
    p3_g.append((ext0, "ext0"))

    def p3_dma(ic):
        g, gn = p3_g[ic % 4]
        p.dma("qg%d" % (ic % 4), "sp", g, sp_gate[ic, :, :],
              reads=["sp_gate"], writes=[gn, "nA", "nB"])

    def p3_round(ic):
        g, gn = p3_g[ic % 4]
        round_split(g, gn, scal[:, 4:5])

    def p3_chunk(ic):
        sl = ic % 3
        g, gn = p3_g[ic % 4]
        g2n = "gst%d" % (3 + sl)
        g2 = gstage[:, 3 + sl, 0:T]
        p.op("ac", lambda e, g=g, g2=g2: e.activation(
            g2, g, AF.Sigmoid, scale=scal[:, 5:6]),
            reads=[gn, "scal"], writes=[g2n])
        p.op("ac", lambda e, g2=g2: e.activation(g2, g2, AF.Copy,
                                                 scale=65536.0, bias=MAGIC),
             reads=[g2n], writes=[g2n])
        p.op("dv", lambda e, g2=g2: e.tensor_scalar(g2, g2, MAGIC, 65535.0,
                                                    ALU.subtract, ALU.min),
             reads=[g2n], writes=[g2n])
        p.op("gp", lambda e, g=g, g2=g2: e.tensor_mul(g2, g2, g),
             reads=[gn, g2n], writes=[g2n])
        p.dma("qspg2", "gp", sp_g2[ic, :, :], g2, reads=[g2n],
              writes=["sp_g2"])

    def p3_red(ic):
        sl = ic % 3
        g2n = "gst%d" % (3 + sl)
        g2 = gstage[:, 3 + sl, 0:T]
        p.op("dv", lambda e, g2=g2: e.tensor_reduce(
            t128e, g2, AX.X, ALU.max, apply_absolute_value=True),
            reads=[g2n], writes=["t_e"])
        p.op("dv", lambda e: e.tensor_max(gmax[:, 3:4], gmax[:, 3:4], t128e),
             reads=["t_e", "gmax"], writes=["gmax"])

    scalar_allreduce([1, 2], 1, 2, pre_emit=lambda: (
        p3_dma(0), p3_dma(1), p3_dma(2), wd_ld(0), wd_ld(1)))
    bcast_scale(0, 2, 3)
    bcast_scale(1, 4, 5)
    p.op("dv", lambda e: e.tensor_scalar_mul(m1x1[0:1, 12:13],
                                             m1x1[0:1, 5:6], 1.0 / 65536.0),
         reads=["m1x1"], writes=["m1x1"])

    p.fence()

    p3_round(0)
    p3_round(1)
    for ic in range(IC):
        p3_chunk(ic)
        wd_tr(ic)
        if ic + 2 < NWU:
            wd_ld(ic + 2)
        if ic > 0:
            p3_red(ic - 1)
        if ic + 2 < IC:
            p3_round(ic + 2)
        if ic + 3 < IC:
            p3_dma(ic + 3)
    p3_red(IC - 1)

    p4a_a = [(gstage[:, i, 0:T], "gst%d" % i) for i in range(3)]
    p4a_a.append((ext0, "ext0"))
    p4a_b = [(gstage[:, 3 + i, 0:T], "gst%d" % (3 + i)) for i in range(3)]
    p4a_b.append((ext1, "ext1"))

    def p4a_dma(ic):
        sl = ic % 4
        (a, an), (b, bn) = p4a_a[sl], p4a_b[sl]
        aw = [an] + (["w4nat0", "w4nat1"] if sl == 3 else [])
        bw = [bn] + (["w4T0", "w4T1"] if sl == 3 else [])
        p.dma("qg%d" % sl, "sp", a, sp_g2[ic, :, :], reads=["sp_g2"],
              writes=aw)
        p.dma("qh%d" % sl, "sp", b, sp_up[ic, :, :], reads=["sp_up"],
              writes=bw)

    def p4a_round(ic):
        sl = ic % 4
        (a, an), (b, bn) = p4a_a[sl], p4a_b[sl]
        round_act(a, an, scal[:, 10:11])
        round_rne(b, bn, scale_ap=scal[:, 2:3], scale_read="scal")

    def p4a_chunk(ic):
        sl = ic % 4
        (a, an), (b, bn) = p4a_a[sl], p4a_b[sl]
        p.op("gp", lambda e, a=a, b=b: e.tensor_mul(a, a, b),
             reads=[an, bn], writes=[an])
        p.dma("qspo", "gp", sp_o[ic, :, :], a, reads=[an], writes=["sp_o"])

    def p4a_red(ic):
        sl = ic % 4
        a, an = p4a_a[sl]
        p.op("dv", lambda e, a=a: e.tensor_reduce(
            t128e, a, AX.X, ALU.max, apply_absolute_value=True),
            reads=[an], writes=["t_e"])
        p.op("dv", lambda e: e.tensor_max(gmax[:, 4:5], gmax[:, 4:5], t128e),
             reads=["t_e", "gmax"], writes=["gmax"])

    scalar_allreduce([3], 2, 1, pre_emit=lambda: (
        p4a_dma(0), p4a_dma(1), p4a_dma(2), p4a_dma(3)))
    bcast_scale(0, 6, 7, pre=12)
    fold_mul(10, 12, 6)

    p4a_round(0)
    for ic in range(IC):
        p4a_chunk(ic)
        if ic > 0:
            p4a_red(ic - 1)
        if ic + 1 < IC:
            p4a_round(ic + 1)
        if ic + 3 < IC and ic >= 1:
            p4a_dma(ic + 3)
    p4a_red(IC - 1)

    def p4_loads(t):
        par = t % 2
        p.dma("qot%d" % par, "sp",
              oTs[par].rearrange("p (i c) -> p i c", c=128),
              sp_o.rearrange("i p (t c) -> p i t c", c=128)[:, :, t, :],
              reads=["sp_o"],
              writes=["oT%d" % par, "gst0", "gst1", "gst2"])
        p.dma("qx2%d" % par, "sp", x2ts[par], sp_x2[:, t, :],
              reads=["sp_x2"], writes=["x2t%d" % par])

    fold_mul(13, 7, 3)
    scalar_allreduce([4], 3, 1, pre_emit=lambda: (
        p4_loads(0), p4_loads(1)))
    bcast_scale(0, 8, 9, pre=13)
    fold_mul(11, 13, 8)

    p.fence()

    def p4_split(t):
        par = t % 2
        oTn = "oT%d" % par
        a2, a2n = a2os[par], "a2o%d" % par
        b2, b2n = b2os[par], "b2o%d" % par
        round_split(oTs[par], oTn, scal[:, 11:12])
        p.op("ac", lambda e, par=par: e.activation(
            scr3, oTs[par], AF.Copy, scale=1.0 / 256.0, bias=MAGIC),
            reads=[oTn], writes=["scr3"])
        p.op("dv", lambda e, a2=a2: e.tensor_scalar(a2, scr3, MAGIC, 256.0,
                                                    ALU.subtract, ALU.mult),
             reads=["scr3"], writes=[a2n])
        p.op("dv", lambda e, par=par, a2=a2, b2=b2: e.scalar_tensor_tensor(
            b2, a2, -1.0, oTs[par], ALU.mult, ALU.add),
            reads=[a2n, oTn], writes=[b2n])

    def p4_mm(t):
        par = t % 2
        x2n = "x2t%d" % par
        a2n, b2n = "a2o%d" % par, "b2o%d" % par
        pk_t = pkw if par == 0 else pvw
        tns = ["pk0", "pk1"] if par == 0 else ["pv0", "pv1"]
        a2v = a2os[par].rearrange("p (i c) -> p i c", c=128)
        b2v = b2os[par].rearrange("p (i c) -> p i c", c=128)
        for half in range(2):
            tgt = pk_t[:, half * 512:(half + 1) * 512]
            fns = []
            for ic in range(IC):
                for pi, (av, wt) in enumerate(
                        [(a2v, wdTh), (a2v, wdTl), (b2v, wdTh)]):
                    fns.append(MM(
                        tgt, av[:, ic, :],
                        wt[:, ic, half * 512:(half + 1) * 512],
                        ic == 0 and pi == 0, ic == IC - 1 and pi == 2))
            p.chain("pe", fns,
                    reads=[a2n, b2n, "wdTh", "wdTl"], writes=[tns[half]])
        p.op("dv", lambda e, par=par, pk_t=pk_t: e.scalar_tensor_tensor(
            out_sb, pk_t, scal[:, 9:10], x2ts[par], ALU.mult, ALU.add),
            reads=[tns[0], tns[1], "scal", x2n], writes=["out_sb"])
        p.dma("qo", "sp",
              y_out.rearrange("(t p) f -> p t f", p=128)[:, t, :], out_sb,
              reads=["out_sb"], writes=["y_done"])

    p4_split(0)
    for t in range(nT):
        if t + 1 < nT:
            p4_split(t + 1)
        p4_mm(t)
        if t + 2 < nT:
            p4_loads(t + 2)

    p.final_wait("sp", "qo")

    with nc.Block() as block:
        @block.sync
        def _(eng):
            p.replay("sp", eng)

        @block.tensor
        def _(eng):
            p.replay("pe", eng)

        @block.vector
        def _(eng):
            p.replay("dv", eng)

        @block.scalar
        def _(eng):
            p.replay("ac", eng)

        @block.gpsimd
        def _(eng):
            p.replay("gp", eng)

    return nc


def kernel(L0_attn_x, L0_attn_Wq, L0_attn_Wk, L0_attn_Wv, L0_attn_Wo,
           L0_attn_bo, L0_ffn_w_gate, L0_ffn_w_up, L0_ffn_w_down):
    world = N_CORES
    x = np.ascontiguousarray(np.asarray(L0_attn_x, dtype=np.float32)
                             .reshape(-1, D))
    Ttot = x.shape[0]
    T = Ttot // world
    base = {
        "wq": np.ascontiguousarray(L0_attn_Wq, dtype=np.float32),
        "wk": np.ascontiguousarray(L0_attn_Wk, dtype=np.float32),
        "wv": np.ascontiguousarray(L0_attn_Wv, dtype=np.float32),
        "wo": np.ascontiguousarray(L0_attn_Wo, dtype=np.float32),
        "bo": np.ascontiguousarray(np.asarray(L0_attn_bo, np.float32)
                                   .reshape(1, D)),
        "wgate": np.ascontiguousarray(L0_ffn_w_gate, dtype=np.float32),
        "wup": np.ascontiguousarray(L0_ffn_w_up, dtype=np.float32),
        "wdown": np.ascontiguousarray(L0_ffn_w_down, dtype=np.float32),
        "ident": np.eye(128, dtype=np.float32),
        "onesc": np.ones((128, 1), np.float32),
        "onesr": np.ones((1, 128), np.float32),
    }
    in_maps = []
    for c in range(world):
        m = dict(base)
        m["x"] = np.ascontiguousarray(x[c * T:(c + 1) * T])
        in_maps.append(m)
    nc = build(T, world)
    res = run_bass_kernel_spmd(nc, in_maps, core_ids=list(range(world)))
    out = np.concatenate([res.results[c]["y"] for c in range(world)], axis=0)
    return out.reshape(1, Ttot, D).astype(np.float32)


# version 52
# speedup vs baseline: 1.0488x; 1.0488x over previous
from contextlib import ExitStack

import numpy as np

import concourse.bass as bass
import concourse.mybir as mybir
from concourse.bass_utils import run_bass_kernel_spmd

F32 = mybir.dt.float32
BF16 = mybir.dt.bfloat16
AF = mybir.ActivationFunctionType
ALU = mybir.AluOpType
AX = mybir.AxisListType

MAGIC = float(np.float32(3 * 2 ** 22))
D = 1024
HC = D // 128
I3 = 3072
IC = I3 // 128
QMAX = 32767.0
WQMAX = 7.0
EPS = 1e-5
N_CORES = 8


class Prog:
    ENGS = ("pe", "dv", "ac", "gp", "sp")

    def __init__(self, nc, es):
        self.nc = nc
        self.es = es
        self.sem = {}
        self.cnt = {}
        for name in ("pe", "dv", "ac", "gp"):
            self.getsem(name)
        self.waited = {e: {} for e in self.ENGS}
        self.buf = {}
        self.q = {e: [] for e in self.ENGS}

    def getsem(self, name):
        if name not in self.sem:
            self.sem[name] = self.es.enter_context(
                self.nc.semaphore("sem_" + name))
            self.cnt[name] = 0
        return self.sem[name]

    def _deps(self, reads, writes):
        toks = []
        for b in reads:
            st = self.buf.get(b)
            if st and st["w"]:
                toks.append(st["w"])
        for b in writes:
            st = self.buf.get(b)
            if st:
                if st["w"]:
                    toks.append(st["w"])
                toks.extend(st["r"])
        return toks

    def _waits_for(self, ename, toks, extra=None):
        w = self.waited[ename]
        need = {}
        for (s, v) in toks:
            if v > w.get(s, 0) and v > need.get(s, 0):
                need[s] = v
        if extra is not None:
            s, v = extra
            if v > w.get(s, 0) and v > need.get(s, 0):
                need[s] = v
        out = sorted(need.items())
        for s, v in out:
            w[s] = v
        return out

    def _record(self, ename, fns, reads, writes, incsem, incn, selfwait=True):
        toks = self._deps(reads, writes)
        extra = None
        if selfwait and incsem is not None and self.cnt[incsem] > 0:
            extra = (incsem, self.cnt[incsem])
        waits = self._waits_for(ename, toks, extra)
        token = None
        if incsem is not None:
            self.cnt[incsem] += incn
            token = (incsem, self.cnt[incsem])
        self.q[ename].append((waits, fns, incsem, incn))
        for b in reads:
            st = self.buf.setdefault(b, {"w": None, "r": []})
            if token:
                st["r"].append(token)
        for b in writes:
            self.buf[b] = {"w": token, "r": []}
        return token

    def op(self, ename, fn, reads=(), writes=()):
        return self._record(ename, [fn], reads, writes, ename, 1,
                            selfwait=False)

    def chain(self, ename, fns, reads=(), writes=()):
        return self._record(ename, fns, reads, writes, ename, 1,
                            selfwait=False)

    def dma(self, qname, ename, out, in_, reads=(), writes=()):
        self.getsem(qname)
        fn = (lambda e, o=out, i=in_: e.dma_start(out=o, in_=i))
        return self._record(ename, [fn], reads, writes, qname, 16,
                            selfwait=False)

    def fence(self):
        toks = [(s, self.cnt[s]) for s in self.sem if self.cnt[s] > 0]
        for e in self.ENGS:
            waits = self._waits_for(e, toks)
            if waits:
                self.q[e].append((waits, [], None, 0))

    def final_wait(self, ename, qname):
        self.q[ename].append(([(qname, self.cnt[qname])], [], None, 0))

    def replay(self, ename, eng):
        for waits, fns, incsem, incn in self.q[ename]:
            for s, v in waits:
                eng.wait_ge(self.sem[s], v)
            insts = [fn(eng) for fn in fns]
            if incsem is not None and insts:
                insts[-1].then_inc(self.sem[incsem], incn)


def build(T, world):
    nT = T // 128
    STW = min(T, 512)
    nST = max(T // 512, 1)
    rg = [list(range(world))]
    nc = bass.Bass("TRN2", target_bir_lowering=False)
    es = ExitStack()

    x_in = nc.dram_tensor("x", [T, D], F32, kind="ExternalInput").ap()
    wq_h = nc.dram_tensor("wq", [D, D], F32, kind="ExternalInput").ap()
    wk_h = nc.dram_tensor("wk", [D, D], F32, kind="ExternalInput").ap()
    wv_h = nc.dram_tensor("wv", [D, D], F32, kind="ExternalInput").ap()
    wo_h = nc.dram_tensor("wo", [D, D], F32, kind="ExternalInput").ap()
    bo_h = nc.dram_tensor("bo", [1, D], F32, kind="ExternalInput").ap()
    wg_h = nc.dram_tensor("wgate", [I3, D], F32, kind="ExternalInput").ap()
    wu_h = nc.dram_tensor("wup", [I3, D], F32, kind="ExternalInput").ap()
    wd_h = nc.dram_tensor("wdown", [D, I3], F32, kind="ExternalInput").ap()
    ident_h = nc.dram_tensor("ident", [128, 128], F32, kind="ExternalInput").ap()
    onesc_h = nc.dram_tensor("onesc", [128, 1], F32, kind="ExternalInput").ap()
    onesr_h = nc.dram_tensor("onesr", [1, 128], F32, kind="ExternalInput").ap()
    y_out = nc.dram_tensor("y", [T, D], F32, kind="ExternalOutput").ap()

    sp_up = nc.dram_tensor("sp_up", [IC, 128, T], F32, kind="Internal").ap()
    sp_gate = nc.dram_tensor("sp_gate", [IC, 128, T], F32, kind="Internal").ap()
    sp_g2 = nc.dram_tensor("sp_g2", [IC, 128, T], F32, kind="Internal").ap()
    sp_o = nc.dram_tensor("sp_o", [IC, 128, T], F32, kind="Internal").ap()
    sp_x2 = nc.dram_tensor("sp_x2", [128, nT, D], F32, kind="Internal").ap()
    sp_wdq = nc.dram_tensor("sp_wdq", [HC * (I3 // D), 128, D], F32,
                            kind="Internal").ap()
    cc_ctx_i = nc.dram_tensor("cc_ctx_i", [128, HC * 1025], F32, kind="Internal").ap()
    cc_ctx_o = nc.dram_tensor("cc_ctx_o", [128, HC * 1025], F32, kind="Internal").ap()
    cc_i = [nc.dram_tensor("cc_i%d" % k, [1, 2], F32, kind="Internal").ap()
            for k in range(4)]
    cc_o = [nc.dram_tensor("cc_o%d" % k, [1, 2], F32, kind="Internal").ap()
            for k in range(4)]

    A = lambda n, s: nc.alloc_sbuf_tensor(n, s, F32).ap()
    ident = A("c_ident", [128, 128])
    onesc = A("c_onesc", [128, 1])
    onesr = A("c_onesr", [1, 128])
    epsc = A("c_eps", [128, 1])
    ln32 = A("c_ln32", [1, 1])
    bo_bc = A("bo_bc", [128, D])
    xts = [A("xt0", [128, D]), A("xt1", [128, D]),
           A("xt2", [128, D]), A("xt3", [128, D])]
    hs = [A("h0", [128, D]), A("h1", [128, D])]
    h = hs[0]
    x2ts = [A("x2t0", [128, D]), A("x2t1", [128, D])]
    x2t = x2ts[0]
    sq_scr = A("sq_scr", [128, 512])
    stats = A("stats", [128, 2 * nT])
    gmax = A("gmax", [128, 8])
    scal = A("scal", [128, 16])
    m1x1 = A("m1x1", [1, 16])
    negc = A("negc", [128, 1])
    negcq = A("negcq", [128, 1])
    t128a = A("t_mean", [128, 1])
    t128b = A("t_inv", [128, 1])
    t128c = A("t_c", [128, 1])
    t128d = A("t_d", [128, 1])
    t128e = A("t_e", [128, 1])
    u128a = A("u_mean", [128, 1])
    u128b = A("u_inv", [128, 1])
    v128a = A("v_a", [128, 1])
    v128b = A("v_b", [128, 1])
    rs8 = A("rs8", [128, 8])
    blk = A("blk", [128, 32])
    blk2 = A("blk2", [128, 32])
    blk3 = A("blk3", [128, 32])
    srow = A("srow", [1, 128])
    wblkA = [A("wblkA0", [128, 32]), A("wblkA1", [128, 32])]
    wblkB = [A("wblkB0", [128, 32]), A("wblkB1", [128, 32])]
    wblkC = [A("wblkC0", [128, 32]), A("wblkC1", [128, 32])]
    ln6 = A("ln6", [128, 12])
    lns2 = A("lns2", [128, 2])
    onesc_bf = nc.alloc_sbuf_tensor("c_onesc_bf", [128, 1], BF16).ap()
    s1x1 = A("s1x1", [1, 8])

    S0 = ((nc.sbuf_base + 63) // 64) * 64
    AT = lambda n, s, off: nc.alloc_sbuf_tensor_at(n, s, F32, offset=S0 + off).ap()
    KB = 1024
    ATB1 = lambda n, s, off: nc.alloc_sbuf_tensor_at(
        n, s, BF16, offset=S0 + off).ap()
    w0h = ATB1("w0h", [128, HC, D], 0)
    w0l = ATB1("w0l", [128, HC, D], 16 * KB)
    w1h = ATB1("w1h", [128, HC, D], 32 * KB)
    w1l = ATB1("w1l", [128, HC, D], 48 * KB)
    wstage = AT("wstage", [128, 2, D], 64 * KB)
    ctx = AT("ctx", [128, HC, 1025], 72 * KB)
    hThs = [ATB1("hTh0", [128, D], 105 * KB), ATB1("hTh1", [128, D], 107 * KB)]
    hTls = [ATB1("hTl0", [128, D], 109 * KB), ATB1("hTl1", [128, D], 111 * KB)]
    ehs = [ATB1("eh0", [128, D], 113 * KB), ATB1("eh1", [128, D], 115 * KB)]
    els = [ATB1("el0", [128, D], 117 * KB), ATB1("el1", [128, D], 119 * KB)]
    vhs = [ATB1("vh0", [128, D], 121 * KB), ATB1("vh1", [128, D], 123 * KB)]
    vls = [ATB1("vl0", [128, D], 125 * KB), ATB1("vl1", [128, D], 127 * KB)]
    ef32 = AT("ef32", [128, D], 129 * KB)
    wslot_x = [AT("wkst%d" % i, [128, D], (113 + 4 * i) * KB)
               for i in range(3)]
    cWh = ATB1("cWh", [128, HC, D], 113 * KB)
    cWl = ATB1("cWl", [128, HC, D], 129 * KB)
    ctTh = ATB1("ctTh", [128, D], 145 * KB)
    ctTl = ATB1("ctTl", [128, D], 147 * KB)
    q_sbs = [AT("q_sb0", [128, D], 149 * KB), AT("q_sb1", [128, D], 153 * KB)]
    qThs = [ATB1("qTh0", [128, D], 157 * KB), ATB1("qTh1", [128, D], 159 * KB)]
    qTls = [ATB1("qTl0", [128, D], 161 * KB), ATB1("qTl1", [128, D], 163 * KB)]
    ATB = lambda n, s, off: nc.alloc_sbuf_tensor_at(
        n, s, BF16, offset=S0 + off).ap()
    nA = ATB("nA", [128, HC, T], 0)
    nB = ATB("nB", [128, HC, T], 32 * KB)
    wnat = AT("wnat", [128, 2, 2, D], 64 * KB)
    wTch = ATB("wTch", [128, 2, 2, HC * 128], 80 * KB)
    wTcl = ATB("wTcl", [128, 2, 2, HC * 128], 88 * KB)
    stage = AT("stage", [128, 4, T], 96 * KB)
    sq_scr2 = AT("sq_scr2", [128, D], 128 * KB)
    p2scr = AT("p2scr", [128, D], 132 * KB)
    gstage = AT("gstage", [128, 6, T], 0)
    wdTh = ATB("wdTh", [128, IC, D], 48 * KB)
    wdTl = ATB("wdTl", [128, IC, D], 96 * KB)
    w4nat = AT("w4nat", [128, 2, D], 144 * KB)
    w4T = AT("w4T", [128, 2, HC * 128], 152 * KB)
    ext0 = AT("ext0", [128, T], 144 * KB)
    ext1 = AT("ext1", [128, T], 152 * KB)
    oTs = [AT("oT0", [128, IC * 128], 0),
           AT("oT1", [128, IC * 128], 12 * KB)]
    scr3 = AT("scr3", [128, IC * 128], 24 * KB)
    a2os = [ATB("a2o0", [128, IC * 128], 36 * KB),
            ATB("a2o1", [128, IC * 128], 144 * KB)]
    b2os = [ATB("b2o0", [128, IC * 128], 42 * KB),
            ATB("b2o1", [128, IC * 128], 150 * KB)]
    out_sb = AT("out_sb", [128, D], 156 * KB)

    PA = lambda n, s: nc.alloc_psum_tensor(n, s, F32).ap()
    ptr8 = PA("ptr8", [128, D])
    pkw = PA("pkw", [128, D])
    pvw = PA("pvw", [128, D])
    pk0, pk1 = pkw[:, 0:512], pkw[:, 512:1024]
    pv0, pv1 = pvw[:, 0:512], pvw[:, 512:1024]
    pc = PA("pc", [128, 512])
    ps8 = PA("ps8", [128, 128])

    p = Prog(nc, es)

    MM = lambda out, l, r, st, sp: (lambda e: e.matmul(
        out, l, r, start=st, stop=sp, skip_group_check=True))
    TR = lambda out, in_: (lambda e: e.transpose(out, in_, ident))

    def transpose8(src, src_name, dst, dst_name):
        p.chain("pe", [TR(ptr8[:, k * 128:(k + 1) * 128],
                          src[:, k * 128:(k + 1) * 128]) for k in range(HC)],
                reads=[src_name, "ident"], writes=["ptr8"])
        p.op("dv", lambda e: e.tensor_copy(dst, ptr8),
             reads=["ptr8"], writes=[dst_name])

    def ln_stats(src_name, src, mo, io, mname, iname):
        p.op("dv", lambda e: e.bn_stats(ln6[:, 0:6], src[:, 0:512]),
             reads=[src_name], writes=["ln6"])
        p.op("dv", lambda e: e.bn_stats(ln6[:, 6:12], src[:, 512:1024]),
             reads=[src_name], writes=["ln6"])
        p.op("dv", lambda e: e.bn_aggr(lns2, ln6),
             reads=["ln6"], writes=["lns2"])
        p.op("ac", lambda e: e.activation(t128c, lns2[:, 1:2], AF.Ln,
                                          bias=epsc),
             reads=["lns2", "c_eps"], writes=["t_c"])
        p.op("ac", lambda e, io=io: e.activation(io, t128c, AF.Exp,
                                                 scale=-0.5),
             reads=["t_c"], writes=[iname])
        if mo is not None:
            p.op("dv", lambda e, mo=mo: e.tensor_copy(mo, lns2[:, 0:1]),
                 reads=["lns2"], writes=[mname])

    def exp_shift(wname, wbuf, dst, dstname):
        for kc in range(HC):
            for half in range(2):
                tgt = pk0 if half == 0 else pk1
                tn = "pk0" if half == 0 else "pk1"
                scr, scrn = ((sq_scr, "sq_scr") if half == 0
                             else (q_sbs[0][:, 0:512], "q_sb0"))
                p.op("ac", lambda e, kc=kc, half=half, scr=scr: e.activation(
                    scr, wbuf[:, kc, half * 512:(half + 1) * 512],
                    AF.Square),
                    reads=[wname], writes=[scrn])
                p.chain("pe", [MM(tgt[0:1, :], onesc, scr,
                                  kc == 0, kc == HC - 1)],
                        reads=[scrn, "c_onesc"], writes=[tn])
        p.op("dv", lambda e: e.reduce_max(s1x1[0:1, 0:1], pk0[0:1, :], AX.X),
             reads=["pk0"], writes=["s1x1"])
        p.op("dv", lambda e: e.reduce_max(s1x1[0:1, 1:2], pk1[0:1, :], AX.X),
             reads=["pk1"], writes=["s1x1"])
        p.op("dv", lambda e: e.tensor_max(s1x1[0:1, 2:3], s1x1[0:1, 0:1],
                                          s1x1[0:1, 1:2]),
             reads=["s1x1"], writes=["s1x1"])
        p.op("ac", lambda e: e.activation(s1x1[0:1, 3:4], s1x1[0:1, 2:3],
                                          AF.Ln, bias=epsc[0:1, :]),
             reads=["s1x1", "c_eps"], writes=["s1x1"])
        p.op("ac", lambda e: e.activation(s1x1[0:1, 4:5], s1x1[0:1, 3:4],
                                          AF.Exp, scale=0.5, bias=ln32),
             reads=["s1x1", "c_ln32"], writes=["s1x1"])
        p.chain("pe", [MM(ps8[:, 0:1], onesr, s1x1[0:1, 4:5], True, True)],
                reads=["s1x1", "c_onesr"], writes=["ps8"])
        p.op("dv", lambda e: e.tensor_scalar_mul(dst, ps8[:, 0:1], -1.0),
             reads=["ps8"], writes=[dstname])

    def round_act(buf, bufname, scale_ap, scale_read="scal"):
        p.op("ac", lambda e: e.activation(buf, buf, AF.Copy, scale=scale_ap,
                                          bias=MAGIC),
             reads=[bufname, scale_read], writes=[bufname])
        p.op("ac", lambda e: e.activation(buf, buf, AF.Copy, bias=-MAGIC),
             reads=[bufname], writes=[bufname])

    def round_rne(buf, bufname, scale_ap=None, scale_read=None):
        reads = [bufname] + ([scale_read] if scale_read else [])
        if scale_ap is not None:
            p.op("dv", lambda e: e.tensor_scalar(buf, buf, scale_ap, MAGIC,
                                                 ALU.mult, ALU.add),
                 reads=reads, writes=[bufname])
        else:
            p.op("dv", lambda e: e.tensor_scalar(buf, buf, MAGIC, None,
                                                 ALU.add, ALU.bypass),
                 reads=reads, writes=[bufname])
        p.op("dv", lambda e: e.tensor_scalar(buf, buf, MAGIC, None,
                                             ALU.subtract, ALU.bypass),
             reads=[bufname], writes=[bufname])

    p.op("dv", lambda e: e.memset(epsc, EPS), writes=["c_eps"])
    p.op("dv", lambda e: e.memset(ln32, float(np.log(np.float32(32.0 * 1.002)))),
         writes=["c_ln32"])
    p.op("dv", lambda e: e.memset(ctx, 0.0), writes=["ctx"])
    p.op("dv", lambda e: e.memset(gmax, 0.0), writes=["gmax"])
    p.op("dv", lambda e: e.memset(onesc_bf, 1.0), writes=["c_onesc_bf"])
    p.dma("qc", "sp", ident, ident_h, writes=["ident"])
    p.dma("qc", "sp", onesc, onesc_h, writes=["c_onesc"])
    p.dma("qc", "sp", onesr, onesr_h, writes=["c_onesr"])
    def load_w_pair(hbm, hbuf, lbuf, base):
        wslots = [(wstage[:, 0, :], "wstage0", []),
                  (wstage[:, 1, :], "wstage1", []),
                  (ef32, "ef32", []),
                  (wslot_x[0], "wkst0", ["e0h", "e1h"]),
                  (wslot_x[1], "wkst1", ["e0l", "e1l"]),
                  (wslot_x[2], "wkst2", ["v0h", "v1h"])]
        ns = len(wslots)
        for c in range(HC):
            ws, wsn, cov = wslots[c % ns]
            p.dma("qws%d" % (c % ns), "sp", ws,
                  hbm[c * 128:(c + 1) * 128, :], writes=[wsn] + cov)
            p.op("ac", lambda e, c=c, ws=ws: e.activation(
                hbuf[:, c, :], ws, AF.Copy),
                reads=[wsn], writes=[base + "h"])
            p.op("dv", lambda e, c=c, ws=ws: e.tensor_tensor(
                lbuf[:, c, :], ws, hbuf[:, c, :], ALU.subtract),
                reads=[wsn, base + "h"], writes=[base + "l"])

    p.dma("qc", "sp", bo_bc[0:1, :], bo_h, writes=["bo_bc"])
    for t0 in range(min(3, nT)):
        p.dma("qxt%d" % t0, "sp", xts[t0],
              x_in.rearrange("(t p) f -> p t f", p=128)[:, t0, :],
              writes=["xt%d" % t0])
    for half in range(2):
        tgt, tn = (pk0, "pk0") if half == 0 else (pk1, "pk1")
        p.chain("pe", [MM(tgt, onesr, bo_bc[0:1, half * 512:(half + 1) * 512],
                          True, True)],
                reads=["bo_bc", "c_onesr"], writes=[tn])
    p.op("dv", lambda e: e.tensor_copy(bo_bc[:, 0:512], pk0),
         reads=["pk0"], writes=["bo_bc"])
    p.op("dv", lambda e: e.tensor_copy(bo_bc[:, 512:1024], pk1),
         reads=["pk1"], writes=["bo_bc"])

    cbanks = [(pc, "pc"), (pv0, "pv0"), (pv1, "pv1"), (pk0, "pk0"),
              (pk1, "pk1")]

    def front_dma(t):
        px = t % 4
        p.dma("qxt%d" % px, "sp", xts[px],
              x_in.rearrange("(t p) f -> p t f", p=128)[:, t, :],
              writes=["xt%d" % px])

    def front_stats(t):
        par, px = t % 2, t % 4
        xn, hn = "xt%d" % px, "h%d" % par
        ln_stats(xn, xts[px], None, t128b, None, "t_inv")
        p.op("dv", lambda e, par=par, px=px: e.tensor_scalar(
            hs[par], xts[px], lns2[:, 0:1], t128b, ALU.subtract, ALU.mult),
            reads=[xn, "lns2", "t_inv"], writes=[hn])

    def front_tr(t):
        par = t % 2
        hn, hTn = "h%d" % par, "hT%d" % par
        p.chain("pe", [TR(ptr8[:, k * 128:(k + 1) * 128],
                          hs[par][:, k * 128:(k + 1) * 128])
                       for k in range(HC)],
                reads=[hn, "ident"], writes=["ptr8"])
        p.op("ac", lambda e, par=par: e.activation(hThs[par], ptr8, AF.Copy),
             reads=["ptr8"], writes=[hTn + "h"])
        p.op("dv", lambda e, par=par: e.tensor_tensor(
            hTls[par], ptr8, hThs[par], ALU.subtract),
            reads=["ptr8", hTn + "h"], writes=[hTn + "l"])

    def front(t):
        front_dma(t)
        front_stats(t)
        front_tr(t)

    def proj3(tgt, par, wh, wl, whn, wln, half):
        fns = []
        for kc in range(HC):
            sl = slice(kc * 128, (kc + 1) * 128)
            ws = (slice(None), kc, slice(half * 512, (half + 1) * 512))
            fns.append(MM(tgt, hThs[par][:, sl], wh[ws], kc == 0, False))
            fns.append(MM(tgt, hTls[par][:, sl], wh[ws], False, False))
            fns.append(MM(tgt, hThs[par][:, sl], wl[ws], False,
                          kc == HC - 1))
        return fns

    def backA(t):
        par = t % 2
        hTn = "hT%d" % par
        en, vn = "e%d" % par, "v%d" % par
        for half in range(2):
            tgt, tn = (pk0, "pk0") if half == 0 else (pk1, "pk1")
            p.chain("pe", proj3(tgt, par, w0h, w0l, "w0h", "w0l", half),
                    reads=[hTn + "h", hTn + "l", "w0h", "w0l"], writes=[tn])
        p.op("ac", lambda e: e.activation(ef32, pkw, AF.Exp, bias=negc),
             reads=["pk0", "pk1", "negc"], writes=["ef32"])
        p.op("ac", lambda e, par=par: e.activation(ehs[par], ef32, AF.Copy),
             reads=["ef32"], writes=[en + "h"])
        p.op("dv", lambda e, par=par: e.tensor_tensor(
            els[par], ef32, ehs[par], ALU.subtract),
            reads=["ef32", en + "h"], writes=[en + "l"])
        for half in range(2):
            tgt, tn = (pv0, "pv0") if half == 0 else (pv1, "pv1")
            p.chain("pe", proj3(tgt, par, w1h, w1l, "w1h", "w1l", half),
                    reads=[hTn + "h", hTn + "l", "w1h", "w1l"], writes=[tn])
        p.op("ac", lambda e, par=par: e.activation(vhs[par], pvw, AF.Copy),
             reads=["pv0", "pv1"], writes=[vn + "h"])
        p.op("dv", lambda e, par=par: e.tensor_tensor(
            vls[par], pvw, vhs[par], ALU.subtract),
            reads=["pv0", "pv1", vn + "h"], writes=[vn + "l"])

    def backA_ctx(t):
        par = t % 2
        en, vn = "e%d" % par, "v%d" % par
        for i in range(16):
            dc, half = i // 2, i % 2
            bank, bn = cbanks[i % len(cbanks)]
            dsl = slice(dc * 128, (dc + 1) * 128)
            hsl = slice(half * 512, (half + 1) * 512)
            p.chain("pe", [
                MM(bank, ehs[par][:, dsl], vhs[par][:, hsl], True, False),
                MM(bank, els[par][:, dsl], vhs[par][:, hsl], False, False),
                MM(bank, ehs[par][:, dsl], vls[par][:, hsl], False, True)],
                reads=[en + "h", en + "l", vn + "h", vn + "l"], writes=[bn])
            p.op("dv", lambda e, dc=dc, half=half, bank=bank: e.tensor_tensor(
                ctx[:, dc, half * 512:(half + 1) * 512],
                ctx[:, dc, half * 512:(half + 1) * 512], bank, ALU.add),
                reads=[bn, "ctx"], writes=["ctx"])
        fns = []
        for dc in range(HC):
            dsl = slice(dc * 128, (dc + 1) * 128)
            fns.append(MM(ps8[:, dc:dc + 1], ehs[par][:, dsl], onesc_bf,
                          True, False))
            fns.append(MM(ps8[:, dc:dc + 1], els[par][:, dsl], onesc_bf,
                          False, True))
        p.chain("pe", fns, reads=[en + "h", en + "l", "c_onesc_bf"],
                writes=["ps8"])
        p.op("dv", lambda e: e.tensor_tensor(
            ctx[:, :, 1024:1025], ctx[:, :, 1024:1025],
            ps8[:, 0:HC].unsqueeze(2), ALU.add),
            reads=["ps8", "ctx"], writes=["ctx"])

    front_stats(0)
    front_tr(0)
    if nT > 1:
        front_stats(1)
        front_tr(1)
    load_w_pair(wk_h, w0h, w0l, "w0")
    exp_shift("w0h", w0h, negc, "negc")
    load_w_pair(wv_h, w1h, w1l, "w1")
    for t in range(nT):
        if t + 2 < nT:
            front_stats(t + 2)
        backA(t)
        if t + 2 < nT:
            front_tr(t + 2)
        backA_ctx(t)
        if t + 3 < nT:
            front_dma(t + 3)

    p.fence()

    p.dma("qcc", "sp", cc_ctx_i, ctx, reads=["ctx"], writes=["cc_ctx_i"])
    p.op("gp", lambda e: e.collective_compute(
        "AllReduce", ALU.add, replica_groups=rg, ins=[cc_ctx_i],
        outs=[cc_ctx_o]), reads=["cc_ctx_i"], writes=["cc_ctx_o"])
    for t0b in range(min(3, nT)):
        front(t0b)
    load_w_pair(wq_h, w0h, w0l, "w0")
    exp_shift("w0h", w0h, negcq, "negcq")

    def emit_ctxwo():
        for dc in range(HC):
            p.chain("pe", [TR(ptr8[:, ec * 128:(ec + 1) * 128],
                              ctx[:, dc, ec * 128:(ec + 1) * 128])
                           for ec in range(HC)],
                    reads=["ctx", "ident"], writes=["ptr8"])
            p.op("ac", lambda e: e.activation(ctTh, ptr8, AF.Copy),
                 reads=["ptr8"], writes=["ctTh"])
            p.op("dv", lambda e: e.tensor_tensor(ctTl, ptr8, ctTh, ALU.subtract),
                 reads=["ptr8", "ctTh"], writes=["ctTl"])
            for half in range(2):
                tgt, tn = (pk0, "pk0") if half == 0 else (pk1, "pk1")
                fns = []
                for ec in range(HC):
                    sl = slice(ec * 128, (ec + 1) * 128)
                    ws = (slice(None), ec, slice(half * 512, (half + 1) * 512))
                    fns.append(MM(tgt, ctTh[:, sl], w1h[ws], ec == 0, False))
                    fns.append(MM(tgt, ctTl[:, sl], w1h[ws], False, False))
                    fns.append(MM(tgt, ctTh[:, sl], w1l[ws], False,
                                  ec == HC - 1))
                p.chain("pe", fns, reads=["ctTh", "ctTl", "w1h", "w1l"],
                        writes=[tn])
            p.op("ac", lambda e, dc=dc: e.activation(
                cWh[:, dc, :], pkw, AF.Copy, scale=rs8[:, dc:dc + 1]),
                reads=["pk0", "pk1", "rs8"], writes=["cWh"])
            p.op("dv", lambda e, dc=dc: e.scalar_tensor_tensor(
                cWl[:, dc, :], pkw, rs8[:, dc:dc + 1], cWh[:, dc, :],
                ALU.mult, ALU.subtract),
                reads=["pk0", "pk1", "rs8", "cWh"], writes=["cWl"])

    p.op("dv", lambda e: e.memset(gmax, 0.0), writes=["gmax"])

    def sq_mm(t):
        par = t % 2
        hTn, qn = "hT%d" % par, "q_sb%d" % par
        for half in range(2):
            tgt, tn = (pk0, "pk0") if half == 0 else (pk1, "pk1")
            p.chain("pe", proj3(tgt, par, w0h, w0l, "w0h", "w0l", half),
                    reads=[hTn + "h", hTn + "l", "w0h", "w0l"], writes=[tn])
        p.op("ac", lambda e, par=par: e.activation(
            q_sbs[par], pkw, AF.Exp, bias=negcq, accum_out=v128a),
            reads=["pk0", "pk1", "negcq"], writes=[qn, "v_a"])
        p.op("dv", lambda e: e.reciprocal(v128a, v128a),
             reads=["v_a"], writes=["v_a"])
        p.op("dv", lambda e, par=par: e.tensor_scalar_mul(
            q_sbs[par], q_sbs[par], v128a),
            reads=[qn, "v_a"], writes=[qn])

    def sq_tr(t):
        par = t % 2
        qn, qTn = "q_sb%d" % par, "qT%d" % par
        p.chain("pe", [TR(ptr8[:, k * 128:(k + 1) * 128],
                          q_sbs[par][:, k * 128:(k + 1) * 128])
                       for k in range(HC)],
                reads=[qn, "ident"], writes=["ptr8"])
        p.op("ac", lambda e, par=par: e.activation(qThs[par], ptr8, AF.Copy),
             reads=["ptr8"], writes=[qTn + "h"])
        p.op("dv", lambda e, par=par: e.tensor_tensor(
            qTls[par], ptr8, qThs[par], ALU.subtract),
            reads=["ptr8", qTn + "h"], writes=[qTn + "l"])

    def restB(t):
        par = t % 2
        qTn, x2n = "qT%d" % par, "x2t%d" % par
        for half in range(2):
            tgt, tn = (pv0, "pv0") if half == 0 else (pv1, "pv1")
            fns = []
            for dc in range(HC):
                sl = slice(dc * 128, (dc + 1) * 128)
                ws = (slice(None), dc, slice(half * 512, (half + 1) * 512))
                fns.append(MM(tgt, qThs[par][:, sl], cWh[ws], dc == 0, False))
                fns.append(MM(tgt, qTls[par][:, sl], cWh[ws], False, False))
                fns.append(MM(tgt, qThs[par][:, sl], cWl[ws], False,
                              dc == HC - 1))
            p.chain("pe", fns,
                    reads=[qTn + "h", qTn + "l", "cWh", "cWl"], writes=[tn])

    def restB_add(t):
        par = t % 2
        x2n = "x2t%d" % par
        p.op("dv", lambda e, par=par: e.tensor_tensor(
            x2ts[par], pvw, bo_bc, ALU.add),
            reads=["pv0", "pv1", "bo_bc"], writes=[x2n])
        p.op("dv", lambda e, par=par, px=t % 4: e.tensor_add(
            x2ts[par], x2ts[par], xts[px]),
            reads=[x2n, "xt%d" % (t % 4)], writes=[x2n])
        p.dma("qsx2", "gp", sp_x2[:, t, :], x2ts[par], reads=[x2n],
              writes=["sp_x2"])

    def restB_stats(t):
        par = t % 2
        x2n = "x2t%d" % par
        ln_stats(x2n, x2ts[par], stats[:, 2 * t:2 * t + 1],
                 stats[:, 2 * t + 1:2 * t + 2], "stats", "stats")
        p.op("dv", lambda e: e.tensor_copy(u128a, lns2[:, 0:1]),
             reads=["lns2"], writes=["u_mean"])
        p.op("dv", lambda e, par=par: e.reduce_max(t128c, x2ts[par], AX.X),
             reads=[x2n], writes=["t_c"])
        p.op("dv", lambda e, par=par: e.tensor_reduce(t128d, x2ts[par], AX.X,
                                                      ALU.min),
             reads=[x2n], writes=["t_d"])
        p.op("dv", lambda e: e.tensor_scalar(t128c, t128c, u128a, None,
                                             ALU.subtract, ALU.bypass),
             reads=["t_c", "u_mean"], writes=["t_c"])
        p.op("dv", lambda e: e.tensor_scalar(t128d, t128d, u128a, -1.0,
                                             ALU.subtract, ALU.mult),
             reads=["t_d", "u_mean"], writes=["t_d"])
        p.op("dv", lambda e: e.tensor_max(t128c, t128c, t128d),
             reads=["t_c", "t_d"], writes=["t_c"])
        p.op("dv", lambda e, t=t: e.tensor_mul(
            t128c, t128c, stats[:, 2 * t + 1:2 * t + 2]),
            reads=["t_c", "stats"], writes=["t_c"])
        p.op("dv", lambda e: e.tensor_max(gmax[:, 0:1], gmax[:, 0:1], t128c),
             reads=["t_c", "gmax"], writes=["gmax"])

    sq_mm(0)
    if nT > 1:
        sq_mm(1)
    sq_tr(0)
    if nT > 1:
        sq_tr(1)
    load_w_pair(wo_h, w1h, w1l, "w1")
    p.dma("qcc", "sp", ctx, cc_ctx_o, reads=["cc_ctx_o"], writes=["ctx"])
    for dc in range(HC):
        p.op("dv", lambda e, dc=dc: e.reciprocal(t128a, ctx[:, dc, 1024:1025]),
             reads=["ctx"], writes=["t_mean"])
        p.op("dv", lambda e, dc=dc: e.tensor_scalar_mul(
            rs8[:, dc:dc + 1], t128a, 0.125 / float(D)),
            reads=["t_mean"], writes=["rs8"])
    emit_ctxwo()
    if nT > 3:
        front_dma(3)
    for t in range(nT):
        if t + 3 < nT:
            front_stats(t + 3)
        restB(t)
        if t + 3 < nT:
            front_tr(t + 3)
        if t + 1 < nT and t > 0:
            sq_tr(t + 1)
        restB_add(t)
        if t + 2 < nT:
            sq_mm(t + 2)
        restB_stats(t)
        if t + 4 < nT:
            front_dma(t + 4)

    def scalar_allreduce(cols, k, nvals, pre_emit=None):
        for j, col in enumerate(cols):
            p.chain("pe", [TR(ptr8[0:1, 0:128], gmax[:, col:col + 1])],
                    reads=["gmax", "ident"], writes=["ptr8"])
            p.op("dv", lambda e: e.tensor_copy(srow, ptr8[0:1, 0:128]),
                 reads=["ptr8"], writes=["srow"])
            p.op("dv", lambda e, j=j: e.reduce_max(s1x1[0:1, j:j + 1], srow,
                                                   AX.X),
                 reads=["srow"], writes=["s1x1"])
        p.dma("qcc", "sp", cc_i[k][0:1, 0:nvals], s1x1[0:1, 0:nvals],
              reads=["s1x1"], writes=["cc_i%d" % k])
        p.op("gp", lambda e, k=k: e.collective_compute(
            "AllReduce", ALU.max, replica_groups=rg,
            ins=[cc_i[k][0:1, 0:nvals]], outs=[cc_o[k][0:1, 0:nvals]]),
            reads=["cc_i%d" % k], writes=["cc_o%d" % k])
        if pre_emit is not None:
            pre_emit()
        p.dma("qcc", "sp", s1x1[0:1, 0:nvals], cc_o[k][0:1, 0:nvals],
              reads=["cc_o%d" % k], writes=["s1x1"])

    def bcast_cols(c0, n):
        p.chain("pe", [MM(ps8[:, 0:n], onesr, m1x1[0:1, c0:c0 + n],
                          True, True)],
                reads=["m1x1", "c_onesr"], writes=["ps8"])
        p.op("dv", lambda e, c0=c0, n=n: e.tensor_copy(
            scal[:, c0:c0 + n], ps8[:, 0:n]),
            reads=["ps8"], writes=["scal"])

    def bcast_scale(src_col, rcol, scol, pre=None):
        if pre is not None:
            p.op("dv", lambda e: e.tensor_mul(
                s1x1[0:1, src_col:src_col + 1],
                s1x1[0:1, src_col:src_col + 1], m1x1[0:1, pre:pre + 1]),
                reads=["s1x1", "m1x1"], writes=["s1x1"])
        p.op("dv", lambda e: e.tensor_scalar(
            m1x1[0:1, scol:scol + 1], s1x1[0:1, src_col:src_col + 1],
            1.0 / QMAX, 1e-12, ALU.mult, ALU.max),
            reads=["s1x1"], writes=["m1x1"])
        p.op("dv", lambda e: e.reciprocal(m1x1[0:1, rcol:rcol + 1],
                                          m1x1[0:1, scol:scol + 1]),
             reads=["m1x1"], writes=["m1x1"])
        bcast_cols(min(rcol, scol), abs(scol - rcol) + 1)

    def fold_mul(dst, a, b):
        p.op("dv", lambda e: e.tensor_mul(
            m1x1[0:1, dst:dst + 1], m1x1[0:1, a:a + 1], m1x1[0:1, b:b + 1]),
            reads=["m1x1"], writes=["m1x1"])
        bcast_cols(dst, 1)

    def p2p_load(t):
        par = t % 2
        p.dma("qx2%d" % par, "sp", x2ts[par], sp_x2[:, t, :],
              reads=["sp_x2"], writes=["x2t%d" % par])
        p.op("dv", lambda e, t=t, par=par: e.tensor_scalar(
            hs[par], x2ts[par], stats[:, 2 * t:2 * t + 1],
            stats[:, 2 * t + 1:2 * t + 2], ALU.subtract, ALU.mult),
            reads=["x2t%d" % par, "stats"], writes=["h%d" % par])

    def p2p_main(t):
        par = t % 2
        hn = "h%d" % par
        round_act(hs[par], hn, scal[:, 0:1])
        p.chain("pe", [TR(ptr8[:, k * 128:(k + 1) * 128],
                          hs[par][:, k * 128:(k + 1) * 128])
                       for k in range(HC)],
                reads=[hn, "ident"], writes=["ptr8"])
        p.op("ac", lambda e: e.activation(p2scr, ptr8, AF.Copy,
                                          scale=1.0 / 256.0, bias=MAGIC),
             reads=["ptr8"], writes=["p2scr"])
        p.op("dv", lambda e, t=t: e.tensor_scalar(
            nA[:, :, t * 128:(t + 1) * 128],
            p2scr.rearrange("p (c k) -> p c k", k=128), MAGIC, 256.0,
            ALU.subtract, ALU.mult),
            reads=["p2scr"], writes=["nA"])
        p.op("dv", lambda e, t=t: e.scalar_tensor_tensor(
            nB[:, :, t * 128:(t + 1) * 128],
            nA[:, :, t * 128:(t + 1) * 128], -1.0,
            ptr8.rearrange("p (c k) -> p c k", k=128), ALU.mult, ALU.add),
            reads=["nA", "ptr8"], writes=["nB"])

    scalar_allreduce([0], 0, 1,
                     pre_emit=lambda: (p2p_load(0), p2p_load(1)))
    bcast_scale(0, 0, 1)

    p.fence()

    for t in range(nT):
        p2p_main(t)
        if t + 2 < nT:
            p2p_load(t + 2)

    def dequant_dve(nat, natname, scr, scrname, eng="dv", eng_rnd=None):
        if eng_rnd is None:
            eng_rnd = eng
        nb = nat.shape[-1] // 32
        bv = lambda a: a.rearrange("p (b k) -> p b k", k=32)
        p.op("dv", lambda e: e.tensor_reduce(
            blk[:, 0:nb], bv(nat), AX.X, ALU.max, apply_absolute_value=True),
            reads=[natname], writes=["blk"])
        p.op("dv", lambda e: e.tensor_scalar_mul(blk2[:, 0:nb], blk[:, 0:nb],
                                                 1.0 / WQMAX),
             reads=["blk"], writes=["blk2"])
        p.op("dv", lambda e: e.scalar_tensor_tensor(
            blk3[:, 0:nb], blk2[:, 0:nb], -WQMAX, blk[:, 0:nb],
            ALU.mult, ALU.add),
            reads=["blk", "blk2"], writes=["blk3"])
        p.op("dv", lambda e: e.scalar_tensor_tensor(
            blk2[:, 0:nb], blk3[:, 0:nb], 1.0 / WQMAX, blk2[:, 0:nb],
            ALU.mult, ALU.add),
            reads=["blk2", "blk3"], writes=["blk2"])
        p.op("dv", lambda e: e.tensor_scalar(blk2[:, 0:nb], blk2[:, 0:nb],
                                             1e-12, None, ALU.max, ALU.bypass),
             reads=["blk2"], writes=["blk2"])
        p.op("dv", lambda e: e.reciprocal(blk3[:, 0:nb], blk2[:, 0:nb]),
             reads=["blk2"], writes=["blk3"])
        p.op("dv", lambda e: e.tensor_mul(blk[:, 0:nb], blk3[:, 0:nb],
                                          blk2[:, 0:nb]),
             reads=["blk2", "blk3"], writes=["blk"])
        p.op("dv", lambda e: e.tensor_scalar(blk[:, 0:nb], blk[:, 0:nb],
                                             2.0, -1.0, ALU.subtract,
                                             ALU.mult),
             reads=["blk"], writes=["blk"])
        p.op("dv", lambda e: e.tensor_mul(blk3[:, 0:nb], blk3[:, 0:nb],
                                          blk[:, 0:nb]),
             reads=["blk", "blk3"], writes=["blk3"])
        p.op(eng, lambda e: e.tensor_tensor(
            bv(scr), bv(nat), blk3[:, 0:nb].unsqueeze(2)
            .broadcast_to([128, nb, 32]), ALU.mult),
            reads=[natname, "blk3"], writes=[scrname])
        p.op(eng_rnd, lambda e: e.tensor_scalar(scr, scr, MAGIC, MAGIC,
                                                ALU.add, ALU.subtract),
             reads=[scrname], writes=[scrname])
        p.op(eng, lambda e: e.tensor_tensor(
            bv(scr), bv(scr), blk2[:, 0:nb].unsqueeze(2)
            .broadcast_to([128, nb, 32]), ALU.mult),
            reads=[scrname, "blk2"], writes=[scrname])

    def dequant_tr(scr, scrname, wT_hi, wT_lo, wT_name):
        p.chain("pe", [TR(ptr8[:, k * 128:(k + 1) * 128],
                          scr[:, k * 128:(k + 1) * 128]) for k in range(HC)],
                reads=[scrname, "ident"], writes=["ptr8"])
        p.op("ac", lambda e: e.activation(wT_hi, ptr8, AF.Copy),
             reads=["ptr8"], writes=[wT_name + "h"])
        p.op("dv", lambda e: e.tensor_tensor(wT_lo, ptr8, wT_hi,
                                             ALU.subtract),
             reads=["ptr8", wT_name + "h"], writes=[wT_name + "l"])

    P2SPILL = [(sp_up, "sp_up", 1), (sp_gate, "sp_gate", 2)]
    p2banks = [(pk0, "pk0"), (pk1, "pk1"), (pv0, "pv0"), (pv1, "pv1")]
    p2cnt = [0]

    def p2_loads(jc):
        par = jc % 2
        p.dma("qwn%d0" % par, "sp", wnat[:, par, 0, :],
              wu_h[jc * 128:(jc + 1) * 128, :], writes=["wnat%d0" % par])
        p.dma("qwn%d1" % par, "sp", wnat[:, par, 1, :],
              wg_h[jc * 128:(jc + 1) * 128, :], writes=["wnat%d1" % par])

    def p2_dequant_dve(jc):
        par = jc % 2
        for wi in range(2):
            scr, scrn = (p2scr, "p2scr") if wi == 0 else (sq_scr2, "sq_scr2")
            dequant_dve(wnat[:, par, wi, :], "wnat%d%d" % (par, wi),
                        scr, scrn)

    def p2_dequant_tr(jc):
        par = jc % 2
        dequant_tr(p2scr, "p2scr", wTch[:, par, 0, :], wTcl[:, par, 0, :],
                   "wT%d0" % par)
        w1h_, w1l_ = wTch[:, par, 1, :], wTcl[:, par, 1, :]
        wn = "wT%d1" % par
        p.chain("pe", [TR(pc[:, k * 128:(k + 1) * 128],
                          sq_scr2[:, k * 128:(k + 1) * 128])
                       for k in range(4)],
                reads=["sq_scr2", "ident"], writes=["pc"])
        p.op("ac", lambda e, w1h_=w1h_: e.activation(
            w1h_[:, 0:512], pc, AF.Copy), reads=["pc"], writes=[wn + "h"])
        p.op("dv", lambda e, w1h_=w1h_, w1l_=w1l_: e.tensor_tensor(
            w1l_[:, 0:512], pc, w1h_[:, 0:512], ALU.subtract),
            reads=["pc", wn + "h"], writes=[wn + "l"])
        p.chain("pe", [TR(ptr8[:, k * 128:(k + 1) * 128],
                          sq_scr2[:, (4 + k) * 128:(5 + k) * 128])
                       for k in range(4)],
                reads=["sq_scr2", "ident"], writes=["ptr8"])
        p.op("ac", lambda e, w1h_=w1h_: e.activation(
            w1h_[:, 512:1024], ptr8[:, 0:512], AF.Copy),
            reads=["ptr8"], writes=[wn + "h"])
        p.op("dv", lambda e, w1h_=w1h_, w1l_=w1l_: e.tensor_tensor(
            w1l_[:, 512:1024], ptr8[:, 0:512], w1h_[:, 512:1024],
            ALU.subtract),
            reads=["ptr8", wn + "h"], writes=[wn + "l"])

    def p2_mms(jc):
        par = jc % 2
        sl2 = (jc % 2) * 2
        for wi, (spill, spname, gcol) in enumerate(P2SPILL):
            wtn = "wT%d%d" % (par, wi)
            stn = "stage%d" % (sl2 + wi)
            for st in range(nST):
                tgt, tn = p2banks[p2cnt[0] % 4]
                p2cnt[0] += 1
                fns = []
                for kc in range(HC):
                    for pi, (wt, na) in enumerate(
                            [(wTch, nA), (wTcl, nA), (wTch, nB)]):
                        fns.append(MM(
                            tgt[:, 0:STW],
                            wt[:, par, wi, kc * 128:(kc + 1) * 128],
                            na[:, kc, st * STW:(st + 1) * STW],
                            kc == 0 and pi == 0,
                            kc == HC - 1 and pi == 2))
                p.chain("pe", fns,
                        reads=[wtn + "h", wtn + "l", "nA", "nB"], writes=[tn])
                p.op("ac", lambda e, tgt=tgt, wi=wi, st=st, sl2=sl2:
                     e.activation(stage[:, sl2 + wi, st * STW:(st + 1) * STW],
                                  tgt[:, 0:STW], AF.Copy, scale=scal[:, 1:2]),
                     reads=[tn, "scal"], writes=[stn])
            p.op("dv", lambda e, wi=wi, sl2=sl2: e.tensor_reduce(
                t128e, stage[:, sl2 + wi, :], AX.X, ALU.max,
                apply_absolute_value=True),
                reads=[stn], writes=["t_e"])
            p.op("dv", lambda e, gcol=gcol: e.tensor_max(
                gmax[:, gcol:gcol + 1], gmax[:, gcol:gcol + 1], t128e),
                reads=["t_e", "gmax"], writes=["gmax"])
            p.dma("qsp%d" % wi, "gp", spill[jc, :, :], stage[:, sl2 + wi, :],
                  reads=[stn], writes=[spname])

    NWU = HC * (I3 // D)

    def wd_dma(u):
        hcp, piece, par = u // 3, u % 3, u % 2
        p.dma("qwd%d" % par, "sp", w4nat[:, par, :],
              wd_h[hcp * 128:(hcp + 1) * 128, piece * D:(piece + 1) * D],
              writes=["w4nat%d" % par])

    def wd_dq1(u):
        par = u % 2
        nat, natname = w4nat[:, par, :], "w4nat%d" % par
        scr, scrname = w4T[:, par, :], "w4T%d" % par
        bA, bB, bC = wblkA[par], wblkB[par], wblkC[par]
        An, Bn, Cn = "wblkA%d" % par, "wblkB%d" % par, "wblkC%d" % par
        bv = lambda a: a.rearrange("p (b k) -> p b k", k=32)
        p.op("dv", lambda e, nat=nat, bC=bC: e.tensor_reduce(
            bC, bv(nat), AX.X, ALU.max, apply_absolute_value=True),
            reads=[natname], writes=[Cn])
        p.op("dv", lambda e, bA=bA, bC=bC: e.tensor_scalar_mul(
            bA, bC, 1.0 / WQMAX), reads=[Cn], writes=[An])
        p.op("dv", lambda e, bA=bA, bB=bB, bC=bC: e.scalar_tensor_tensor(
            bB, bA, -WQMAX, bC, ALU.mult, ALU.add),
            reads=[An, Cn], writes=[Bn])
        p.op("dv", lambda e, bA=bA, bB=bB: e.scalar_tensor_tensor(
            bA, bB, 1.0 / WQMAX, bA, ALU.mult, ALU.add),
            reads=[An, Bn], writes=[An])
        p.op("dv", lambda e, bA=bA: e.tensor_scalar(
            bA, bA, 1e-12, None, ALU.max, ALU.bypass),
            reads=[An], writes=[An])
        p.op("dv", lambda e, bA=bA, bB=bB: e.reciprocal(bB, bA),
             reads=[An], writes=[Bn])
        p.op("dv", lambda e, bA=bA, bB=bB, bC=bC: e.tensor_mul(bC, bB, bA),
             reads=[An, Bn], writes=[Cn])
        p.op("dv", lambda e, bC=bC: e.tensor_scalar(
            bC, bC, 2.0, -1.0, ALU.subtract, ALU.mult),
            reads=[Cn], writes=[Cn])
        p.op("dv", lambda e, bB=bB, bC=bC: e.tensor_mul(bB, bB, bC),
             reads=[Bn, Cn], writes=[Bn])
        p.op("gp", lambda e, nat=nat, scr=scr, bB=bB: e.tensor_tensor(
            bv(scr), bv(nat), bB.unsqueeze(2).broadcast_to([128, 32, 32]),
            ALU.mult), reads=[natname, Bn], writes=[scrname])

    def wd_dq2(u):
        par = u % 2
        scr, scrname = w4T[:, par, :], "w4T%d" % par
        bA, An = wblkA[par], "wblkA%d" % par
        bv = lambda a: a.rearrange("p (b k) -> p b k", k=32)
        p.op("dv", lambda e, scr=scr: e.tensor_scalar(
            scr, scr, MAGIC, MAGIC, ALU.add, ALU.subtract),
            reads=[scrname], writes=[scrname])
        p.op("gp", lambda e, scr=scr, bA=bA: e.tensor_tensor(
            bv(scr), bv(scr), bA.unsqueeze(2).broadcast_to([128, 32, 32]),
            ALU.mult), reads=[scrname, An], writes=[scrname])

    def wd_spill(u):
        par = u % 2
        p.dma("qwq%d" % par, "sp", sp_wdq[u, :, :], w4T[:, par, :],
              reads=["w4T%d" % par], writes=["sp_wdq"])

    p2_loads(0)
    p2_loads(1)
    p2_dequant_dve(0)
    p2_dequant_tr(0)
    wd_dma(0)
    for jc in range(IC):
        if jc + 2 < IC:
            p2_loads(jc + 2)
        if jc + 1 < IC:
            p2_dequant_dve(jc + 1)
        p2_mms(jc)
        if jc + 1 < IC:
            p2_dequant_tr(jc + 1)
        if jc + 1 < NWU:
            wd_dma(jc + 1)
        wd_dq1(jc)
        wd_dq2(jc)
        wd_spill(jc)

    def wd_ld(u):
        par = u % 2
        p.dma("qwd%d" % par, "sp", w4T[:, par, :], sp_wdq[u, :, :],
              reads=["sp_wdq"], writes=["w4T%d" % par])

    def wd_tr(u):
        hcp, piece, par = u // 3, u % 3, u % 2
        p.chain("pe", [TR(ptr8[:, k * 128:(k + 1) * 128],
                          w4T[:, par, k * 128:(k + 1) * 128])
                       for k in range(HC)],
                reads=["w4T%d" % par, "ident"], writes=["ptr8"])
        dsth = wdTh[:, piece * HC:(piece + 1) * HC,
                    hcp * 128:(hcp + 1) * 128]
        dstl = wdTl[:, piece * HC:(piece + 1) * HC,
                    hcp * 128:(hcp + 1) * 128]
        pview = ptr8.rearrange("p (c k) -> p c k", k=128)
        p.op("ac", lambda e, dsth=dsth, pview=pview: e.activation(
            dsth, pview, AF.Copy),
            reads=["ptr8"], writes=["wdTh"])
        p.op("dv", lambda e, dsth=dsth, dstl=dstl, pview=pview:
             e.tensor_tensor(dstl, pview, dsth, ALU.subtract),
             reads=["ptr8", "wdTh"], writes=["wdTl"])

    def round_split(buf, bufname, scale_ap):
        p.op("ac", lambda e, buf=buf: e.activation(buf, buf, AF.Copy,
                                                   scale=scale_ap, bias=MAGIC),
             reads=[bufname, "scal"], writes=[bufname])
        p.op("dv", lambda e, buf=buf: e.tensor_scalar(buf, buf, MAGIC, None,
                                                      ALU.subtract,
                                                      ALU.bypass),
             reads=[bufname], writes=[bufname])

    p3_g = [(gstage[:, i, 0:T], "gst%d" % i) for i in range(3)]
    p3_g.append((ext0, "ext0"))

    def p3_dma(ic):
        g, gn = p3_g[ic % 4]
        p.dma("qg%d" % (ic % 4), "sp", g, sp_gate[ic, :, :],
              reads=["sp_gate"], writes=[gn, "nA", "nB"])

    def p3_round(ic):
        g, gn = p3_g[ic % 4]
        round_split(g, gn, scal[:, 4:5])

    def p3_chunk(ic):
        sl = ic % 3
        g, gn = p3_g[ic % 4]
        g2n = "gst%d" % (3 + sl)
        g2 = gstage[:, 3 + sl, 0:T]
        p.op("ac", lambda e, g=g, g2=g2: e.activation(
            g2, g, AF.Sigmoid, scale=scal[:, 5:6]),
            reads=[gn, "scal"], writes=[g2n])
        p.op("ac", lambda e, g2=g2: e.activation(g2, g2, AF.Copy,
                                                 scale=65536.0, bias=MAGIC),
             reads=[g2n], writes=[g2n])
        p.op("dv", lambda e, g2=g2: e.tensor_scalar(g2, g2, MAGIC, 65535.0,
                                                    ALU.subtract, ALU.min),
             reads=[g2n], writes=[g2n])
        p.op("gp", lambda e, g=g, g2=g2: e.tensor_mul(g2, g2, g),
             reads=[gn, g2n], writes=[g2n])
        p.dma("qspg2", "gp", sp_g2[ic, :, :], g2, reads=[g2n],
              writes=["sp_g2"])

    def p3_red(ic):
        sl = ic % 3
        g2n = "gst%d" % (3 + sl)
        g2 = gstage[:, 3 + sl, 0:T]
        p.op("dv", lambda e, g2=g2: e.tensor_reduce(
            t128e, g2, AX.X, ALU.max, apply_absolute_value=True),
            reads=[g2n], writes=["t_e"])
        p.op("dv", lambda e: e.tensor_max(gmax[:, 3:4], gmax[:, 3:4], t128e),
             reads=["t_e", "gmax"], writes=["gmax"])

    scalar_allreduce([1, 2], 1, 2, pre_emit=lambda: (
        p3_dma(0), p3_dma(1), p3_dma(2), wd_ld(0), wd_ld(1)))
    bcast_scale(0, 2, 3)
    bcast_scale(1, 4, 5)
    p.op("dv", lambda e: e.tensor_scalar_mul(m1x1[0:1, 12:13],
                                             m1x1[0:1, 5:6], 1.0 / 65536.0),
         reads=["m1x1"], writes=["m1x1"])

    p.fence()

    p3_round(0)
    p3_round(1)
    for ic in range(IC):
        p3_chunk(ic)
        wd_tr(ic)
        if ic + 2 < NWU:
            wd_ld(ic + 2)
        if ic > 0:
            p3_red(ic - 1)
        if ic + 2 < IC:
            p3_round(ic + 2)
        if ic + 3 < IC:
            p3_dma(ic + 3)
    p3_red(IC - 1)

    p4a_a = [(gstage[:, i, 0:T], "gst%d" % i) for i in range(3)]
    p4a_a.append((ext0, "ext0"))
    p4a_b = [(gstage[:, 3 + i, 0:T], "gst%d" % (3 + i)) for i in range(3)]
    p4a_b.append((ext1, "ext1"))

    def p4a_dma(ic):
        sl = ic % 4
        (a, an), (b, bn) = p4a_a[sl], p4a_b[sl]
        aw = [an] + (["w4nat0", "w4nat1"] if sl == 3 else [])
        bw = [bn] + (["w4T0", "w4T1"] if sl == 3 else [])
        p.dma("qg%d" % sl, "sp", a, sp_g2[ic, :, :], reads=["sp_g2"],
              writes=aw)
        p.dma("qh%d" % sl, "sp", b, sp_up[ic, :, :], reads=["sp_up"],
              writes=bw)

    def p4a_round(ic):
        sl = ic % 4
        (a, an), (b, bn) = p4a_a[sl], p4a_b[sl]
        round_act(a, an, scal[:, 10:11])
        round_rne(b, bn, scale_ap=scal[:, 2:3], scale_read="scal")

    def p4a_chunk(ic):
        sl = ic % 4
        (a, an), (b, bn) = p4a_a[sl], p4a_b[sl]
        p.op("gp", lambda e, a=a, b=b: e.tensor_mul(a, a, b),
             reads=[an, bn], writes=[an])
        p.dma("qspo", "gp", sp_o[ic, :, :], a, reads=[an], writes=["sp_o"])

    def p4a_red(ic):
        sl = ic % 4
        a, an = p4a_a[sl]
        p.op("dv", lambda e, a=a: e.tensor_reduce(
            t128e, a, AX.X, ALU.max, apply_absolute_value=True),
            reads=[an], writes=["t_e"])
        p.op("dv", lambda e: e.tensor_max(gmax[:, 4:5], gmax[:, 4:5], t128e),
             reads=["t_e", "gmax"], writes=["gmax"])

    scalar_allreduce([3], 2, 1, pre_emit=lambda: (
        p4a_dma(0), p4a_dma(1), p4a_dma(2), p4a_dma(3)))
    bcast_scale(0, 6, 7, pre=12)
    fold_mul(10, 12, 6)

    p4a_round(0)
    for ic in range(IC):
        p4a_chunk(ic)
        if ic > 0:
            p4a_red(ic - 1)
        if ic + 1 < IC:
            p4a_round(ic + 1)
        if ic + 3 < IC and ic >= 1:
            p4a_dma(ic + 3)
    p4a_red(IC - 1)

    def p4_loads(t):
        par = t % 2
        p.dma("qot%d" % par, "sp",
              oTs[par].rearrange("p (i c) -> p i c", c=128),
              sp_o.rearrange("i p (t c) -> p i t c", c=128)[:, :, t, :],
              reads=["sp_o"],
              writes=["oT%d" % par, "gst0", "gst1", "gst2"])
        p.dma("qx2%d" % par, "sp", x2ts[par], sp_x2[:, t, :],
              reads=["sp_x2"], writes=["x2t%d" % par])

    fold_mul(13, 7, 3)
    scalar_allreduce([4], 3, 1, pre_emit=lambda: (
        p4_loads(0), p4_loads(1)))
    bcast_scale(0, 8, 9, pre=13)
    fold_mul(11, 13, 8)

    p.fence()

    def p4_split(t):
        par = t % 2
        oTn = "oT%d" % par
        a2, a2n = a2os[par], "a2o%d" % par
        b2, b2n = b2os[par], "b2o%d" % par
        round_split(oTs[par], oTn, scal[:, 11:12])
        p.op("ac", lambda e, par=par: e.activation(
            scr3, oTs[par], AF.Copy, scale=1.0 / 256.0, bias=MAGIC),
            reads=[oTn], writes=["scr3"])
        p.op("dv", lambda e, a2=a2: e.tensor_scalar(a2, scr3, MAGIC, 256.0,
                                                    ALU.subtract, ALU.mult),
             reads=["scr3"], writes=[a2n])
        p.op("dv", lambda e, par=par, a2=a2, b2=b2: e.scalar_tensor_tensor(
            b2, a2, -1.0, oTs[par], ALU.mult, ALU.add),
            reads=[a2n, oTn], writes=[b2n])

    def p4_mm(t):
        par = t % 2
        x2n = "x2t%d" % par
        a2n, b2n = "a2o%d" % par, "b2o%d" % par
        pk_t = pkw if par == 0 else pvw
        tns = ["pk0", "pk1"] if par == 0 else ["pv0", "pv1"]
        a2v = a2os[par].rearrange("p (i c) -> p i c", c=128)
        b2v = b2os[par].rearrange("p (i c) -> p i c", c=128)
        for half in range(2):
            tgt = pk_t[:, half * 512:(half + 1) * 512]
            fns = []
            for ic in range(IC):
                for pi, (av, wt) in enumerate(
                        [(a2v, wdTh), (a2v, wdTl), (b2v, wdTh)]):
                    fns.append(MM(
                        tgt, av[:, ic, :],
                        wt[:, ic, half * 512:(half + 1) * 512],
                        ic == 0 and pi == 0, ic == IC - 1 and pi == 2))
            p.chain("pe", fns,
                    reads=[a2n, b2n, "wdTh", "wdTl"], writes=[tns[half]])
        p.op("dv", lambda e, par=par, pk_t=pk_t: e.scalar_tensor_tensor(
            out_sb, pk_t, scal[:, 9:10], x2ts[par], ALU.mult, ALU.add),
            reads=[tns[0], tns[1], "scal", x2n], writes=["out_sb"])
        p.dma("qo", "sp",
              y_out.rearrange("(t p) f -> p t f", p=128)[:, t, :], out_sb,
              reads=["out_sb"], writes=["y_done"])

    p4_split(0)
    for t in range(nT):
        if t + 1 < nT:
            p4_split(t + 1)
        p4_mm(t)
        if t + 2 < nT:
            p4_loads(t + 2)

    p.final_wait("sp", "qo")

    with nc.Block() as block:
        @block.sync
        def _(eng):
            p.replay("sp", eng)

        @block.tensor
        def _(eng):
            p.replay("pe", eng)

        @block.vector
        def _(eng):
            p.replay("dv", eng)

        @block.scalar
        def _(eng):
            p.replay("ac", eng)

        @block.gpsimd
        def _(eng):
            p.replay("gp", eng)

    return nc


def kernel(L0_attn_x, L0_attn_Wq, L0_attn_Wk, L0_attn_Wv, L0_attn_Wo,
           L0_attn_bo, L0_ffn_w_gate, L0_ffn_w_up, L0_ffn_w_down):
    world = N_CORES
    x = np.ascontiguousarray(np.asarray(L0_attn_x, dtype=np.float32)
                             .reshape(-1, D))
    Ttot = x.shape[0]
    T = Ttot // world
    base = {
        "wq": np.ascontiguousarray(L0_attn_Wq, dtype=np.float32),
        "wk": np.ascontiguousarray(L0_attn_Wk, dtype=np.float32),
        "wv": np.ascontiguousarray(L0_attn_Wv, dtype=np.float32),
        "wo": np.ascontiguousarray(L0_attn_Wo, dtype=np.float32),
        "bo": np.ascontiguousarray(np.asarray(L0_attn_bo, np.float32)
                                   .reshape(1, D)),
        "wgate": np.ascontiguousarray(L0_ffn_w_gate, dtype=np.float32),
        "wup": np.ascontiguousarray(L0_ffn_w_up, dtype=np.float32),
        "wdown": np.ascontiguousarray(L0_ffn_w_down, dtype=np.float32),
        "ident": np.eye(128, dtype=np.float32),
        "onesc": np.ones((128, 1), np.float32),
        "onesr": np.ones((1, 128), np.float32),
    }
    in_maps = []
    for c in range(world):
        m = dict(base)
        m["x"] = np.ascontiguousarray(x[c * T:(c + 1) * T])
        in_maps.append(m)
    nc = build(T, world)
    res = run_bass_kernel_spmd(nc, in_maps, core_ids=list(range(world)))
    out = np.concatenate([res.results[c]["y"] for c in range(world)], axis=0)
    return out.reshape(1, Ttot, D).astype(np.float32)


# version 53
# speedup vs baseline: 1.0526x; 1.0037x over previous
from contextlib import ExitStack

import numpy as np

import concourse.bass as bass
import concourse.mybir as mybir
from concourse.bass_utils import run_bass_kernel_spmd

F32 = mybir.dt.float32
BF16 = mybir.dt.bfloat16
AF = mybir.ActivationFunctionType
ALU = mybir.AluOpType
AX = mybir.AxisListType

MAGIC = float(np.float32(3 * 2 ** 22))
D = 1024
HC = D // 128
I3 = 3072
IC = I3 // 128
QMAX = 32767.0
WQMAX = 7.0
EPS = 1e-5
N_CORES = 8


class Prog:
    ENGS = ("pe", "dv", "ac", "gp", "sp")

    def __init__(self, nc, es):
        self.nc = nc
        self.es = es
        self.sem = {}
        self.cnt = {}
        for name in ("pe", "dv", "ac", "gp"):
            self.getsem(name)
        self.waited = {e: {} for e in self.ENGS}
        self.buf = {}
        self.q = {e: [] for e in self.ENGS}

    def getsem(self, name):
        if name not in self.sem:
            self.sem[name] = self.es.enter_context(
                self.nc.semaphore("sem_" + name))
            self.cnt[name] = 0
        return self.sem[name]

    def _deps(self, reads, writes):
        toks = []
        for b in reads:
            st = self.buf.get(b)
            if st and st["w"]:
                toks.append(st["w"])
        for b in writes:
            st = self.buf.get(b)
            if st:
                if st["w"]:
                    toks.append(st["w"])
                toks.extend(st["r"])
        return toks

    def _waits_for(self, ename, toks, extra=None):
        w = self.waited[ename]
        need = {}
        for (s, v) in toks:
            if v > w.get(s, 0) and v > need.get(s, 0):
                need[s] = v
        if extra is not None:
            s, v = extra
            if v > w.get(s, 0) and v > need.get(s, 0):
                need[s] = v
        out = sorted(need.items())
        for s, v in out:
            w[s] = v
        return out

    def _record(self, ename, fns, reads, writes, incsem, incn, selfwait=True):
        toks = self._deps(reads, writes)
        extra = None
        if selfwait and incsem is not None and self.cnt[incsem] > 0:
            extra = (incsem, self.cnt[incsem])
        waits = self._waits_for(ename, toks, extra)
        token = None
        if incsem is not None:
            self.cnt[incsem] += incn
            token = (incsem, self.cnt[incsem])
        self.q[ename].append((waits, fns, incsem, incn))
        for b in reads:
            st = self.buf.setdefault(b, {"w": None, "r": []})
            if token:
                st["r"].append(token)
        for b in writes:
            self.buf[b] = {"w": token, "r": []}
        return token

    def op(self, ename, fn, reads=(), writes=()):
        return self._record(ename, [fn], reads, writes, ename, 1,
                            selfwait=False)

    def chain(self, ename, fns, reads=(), writes=()):
        return self._record(ename, fns, reads, writes, ename, 1,
                            selfwait=False)

    def dma(self, qname, ename, out, in_, reads=(), writes=()):
        self.getsem(qname)
        fn = (lambda e, o=out, i=in_: e.dma_start(out=o, in_=i))
        return self._record(ename, [fn], reads, writes, qname, 16,
                            selfwait=False)

    def fence(self):
        toks = [(s, self.cnt[s]) for s in self.sem if self.cnt[s] > 0]
        for e in self.ENGS:
            waits = self._waits_for(e, toks)
            if waits:
                self.q[e].append((waits, [], None, 0))

    def final_wait(self, ename, qname):
        self.q[ename].append(([(qname, self.cnt[qname])], [], None, 0))

    def finalize(self):
        comp = ("pe", "dv", "ac", "gp", "sp")
        needed = {}
        for e in self.ENGS:
            for waits, fns, incsem, incn in self.q[e]:
                for s, v in waits:
                    needed.setdefault(s, set()).add(v)
        remap = {}
        for s in comp:
            if s not in self.sem:
                continue
            nd = needed.get(s, set())
            m, old, new = {}, 0, 0
            for e in self.ENGS:
                pass
            q2 = []
            for waits, fns, incsem, incn in self.q[s]:
                if incsem == s and incn > 0:
                    old += incn
                    if old in nd:
                        new += incn
                        m[old] = new
                        q2.append((waits, fns, incsem, incn))
                    else:
                        q2.append((waits, fns, None, 0))
                else:
                    q2.append((waits, fns, incsem, incn))
            self.q[s] = q2
            remap[s] = m
        for e in self.ENGS:
            q2 = []
            for waits, fns, incsem, incn in self.q[e]:
                w2 = []
                for s, v in waits:
                    if s in remap:
                        w2.append((s, remap[s][v]))
                    else:
                        w2.append((s, v))
                q2.append((w2, fns, incsem, incn))
            self.q[e] = q2

    def replay(self, ename, eng):
        for waits, fns, incsem, incn in self.q[ename]:
            for s, v in waits:
                eng.wait_ge(self.sem[s], v)
            insts = [fn(eng) for fn in fns]
            if incsem is not None and insts:
                insts[-1].then_inc(self.sem[incsem], incn)


def build(T, world):
    nT = T // 128
    STW = min(T, 512)
    nST = max(T // 512, 1)
    rg = [list(range(world))]
    nc = bass.Bass("TRN2", target_bir_lowering=False)
    es = ExitStack()

    x_in = nc.dram_tensor("x", [T, D], F32, kind="ExternalInput").ap()
    wq_h = nc.dram_tensor("wq", [D, D], F32, kind="ExternalInput").ap()
    wk_h = nc.dram_tensor("wk", [D, D], F32, kind="ExternalInput").ap()
    wv_h = nc.dram_tensor("wv", [D, D], F32, kind="ExternalInput").ap()
    wo_h = nc.dram_tensor("wo", [D, D], F32, kind="ExternalInput").ap()
    bo_h = nc.dram_tensor("bo", [1, D], F32, kind="ExternalInput").ap()
    wg_h = nc.dram_tensor("wgate", [I3, D], F32, kind="ExternalInput").ap()
    wu_h = nc.dram_tensor("wup", [I3, D], F32, kind="ExternalInput").ap()
    wd_h = nc.dram_tensor("wdown", [D, I3], F32, kind="ExternalInput").ap()
    ident_h = nc.dram_tensor("ident", [128, 128], F32, kind="ExternalInput").ap()
    onesc_h = nc.dram_tensor("onesc", [128, 1], F32, kind="ExternalInput").ap()
    onesr_h = nc.dram_tensor("onesr", [1, 128], F32, kind="ExternalInput").ap()
    y_out = nc.dram_tensor("y", [T, D], F32, kind="ExternalOutput").ap()

    sp_up = nc.dram_tensor("sp_up", [IC, 128, T], F32, kind="Internal").ap()
    sp_gate = nc.dram_tensor("sp_gate", [IC, 128, T], F32, kind="Internal").ap()
    sp_g2 = nc.dram_tensor("sp_g2", [IC, 128, T], F32, kind="Internal").ap()
    sp_o = nc.dram_tensor("sp_o", [IC, 128, T], F32, kind="Internal").ap()
    sp_x2 = nc.dram_tensor("sp_x2", [128, nT, D], F32, kind="Internal").ap()
    sp_wdq = nc.dram_tensor("sp_wdq", [HC * (I3 // D), 128, D], F32,
                            kind="Internal").ap()
    cc_ctx_i = nc.dram_tensor("cc_ctx_i", [128, HC * 1025], F32, kind="Internal").ap()
    cc_ctx_o = nc.dram_tensor("cc_ctx_o", [128, HC * 1025], F32, kind="Internal").ap()
    cc_i = [nc.dram_tensor("cc_i%d" % k, [1, 2], F32, kind="Internal").ap()
            for k in range(4)]
    cc_o = [nc.dram_tensor("cc_o%d" % k, [1, 2], F32, kind="Internal").ap()
            for k in range(4)]

    A = lambda n, s: nc.alloc_sbuf_tensor(n, s, F32).ap()
    ident = A("c_ident", [128, 128])
    onesc = A("c_onesc", [128, 1])
    onesr = A("c_onesr", [1, 128])
    epsc = A("c_eps", [128, 1])
    ln32 = A("c_ln32", [1, 1])
    bo_bc = A("bo_bc", [128, D])
    xts = [A("xt0", [128, D]), A("xt1", [128, D]),
           A("xt2", [128, D]), A("xt3", [128, D])]
    hs = [A("h0", [128, D]), A("h1", [128, D])]
    h = hs[0]
    x2ts = [A("x2t0", [128, D]), A("x2t1", [128, D])]
    x2t = x2ts[0]
    sq_scr = A("sq_scr", [128, 512])
    stats = A("stats", [128, 2 * nT])
    gmax = A("gmax", [128, 8])
    scal = A("scal", [128, 16])
    m1x1 = A("m1x1", [1, 16])
    negc = A("negc", [128, 1])
    negcq = A("negcq", [128, 1])
    t128a = A("t_mean", [128, 1])
    t128b = A("t_inv", [128, 1])
    t128c = A("t_c", [128, 1])
    t128d = A("t_d", [128, 1])
    t128e = A("t_e", [128, 1])
    u128a = A("u_mean", [128, 1])
    u128b = A("u_inv", [128, 1])
    v128a = A("v_a", [128, 1])
    v128b = A("v_b", [128, 1])
    rs8 = A("rs8", [128, 8])
    blk = A("blk", [128, 32])
    blk2 = A("blk2", [128, 32])
    blk3 = A("blk3", [128, 32])
    srow = A("srow", [1, 128])
    wblkA = [A("wblkA0", [128, 32]), A("wblkA1", [128, 32])]
    wblkB = [A("wblkB0", [128, 32]), A("wblkB1", [128, 32])]
    wblkC = [A("wblkC0", [128, 32]), A("wblkC1", [128, 32])]
    ln6 = A("ln6", [128, 12])
    lns2 = A("lns2", [128, 2])
    onesc_bf = nc.alloc_sbuf_tensor("c_onesc_bf", [128, 1], BF16).ap()
    s1x1 = A("s1x1", [1, 8])

    S0 = ((nc.sbuf_base + 63) // 64) * 64
    AT = lambda n, s, off: nc.alloc_sbuf_tensor_at(n, s, F32, offset=S0 + off).ap()
    KB = 1024
    ATB1 = lambda n, s, off: nc.alloc_sbuf_tensor_at(
        n, s, BF16, offset=S0 + off).ap()
    w0h = ATB1("w0h", [128, HC, D], 0)
    w0l = ATB1("w0l", [128, HC, D], 16 * KB)
    w1h = ATB1("w1h", [128, HC, D], 32 * KB)
    w1l = ATB1("w1l", [128, HC, D], 48 * KB)
    wstage = AT("wstage", [128, 2, D], 64 * KB)
    ctx = AT("ctx", [128, HC, 1025], 72 * KB)
    hThs = [ATB1("hTh0", [128, D], 105 * KB), ATB1("hTh1", [128, D], 107 * KB)]
    hTls = [ATB1("hTl0", [128, D], 109 * KB), ATB1("hTl1", [128, D], 111 * KB)]
    ehs = [ATB1("eh0", [128, D], 113 * KB), ATB1("eh1", [128, D], 115 * KB)]
    els = [ATB1("el0", [128, D], 117 * KB), ATB1("el1", [128, D], 119 * KB)]
    vhs = [ATB1("vh0", [128, D], 121 * KB), ATB1("vh1", [128, D], 123 * KB)]
    vls = [ATB1("vl0", [128, D], 125 * KB), ATB1("vl1", [128, D], 127 * KB)]
    ef32 = AT("ef32", [128, D], 129 * KB)
    wslot_x = [AT("wkst%d" % i, [128, D], (113 + 4 * i) * KB)
               for i in range(3)]
    cWh = ATB1("cWh", [128, HC, D], 113 * KB)
    cWl = ATB1("cWl", [128, HC, D], 129 * KB)
    ctTh = ATB1("ctTh", [128, D], 145 * KB)
    ctTl = ATB1("ctTl", [128, D], 147 * KB)
    q_sbs = [AT("q_sb0", [128, D], 149 * KB), AT("q_sb1", [128, D], 153 * KB)]
    qThs = [ATB1("qTh0", [128, D], 157 * KB), ATB1("qTh1", [128, D], 159 * KB)]
    qTls = [ATB1("qTl0", [128, D], 161 * KB), ATB1("qTl1", [128, D], 163 * KB)]
    ATB = lambda n, s, off: nc.alloc_sbuf_tensor_at(
        n, s, BF16, offset=S0 + off).ap()
    nA = ATB("nA", [128, HC, T], 0)
    nB = ATB("nB", [128, HC, T], 32 * KB)
    wnat = AT("wnat", [128, 2, 2, D], 64 * KB)
    wTch = ATB("wTch", [128, 2, 2, HC * 128], 80 * KB)
    wTcl = ATB("wTcl", [128, 2, 2, HC * 128], 88 * KB)
    stage = AT("stage", [128, 4, T], 96 * KB)
    sq_scr2 = AT("sq_scr2", [128, D], 128 * KB)
    p2scr = AT("p2scr", [128, D], 132 * KB)
    gstage = AT("gstage", [128, 6, T], 0)
    wdTh = ATB("wdTh", [128, IC, D], 48 * KB)
    wdTl = ATB("wdTl", [128, IC, D], 96 * KB)
    w4nat = AT("w4nat", [128, 2, D], 144 * KB)
    w4T = AT("w4T", [128, 2, HC * 128], 152 * KB)
    ext0 = AT("ext0", [128, T], 144 * KB)
    ext1 = AT("ext1", [128, T], 152 * KB)
    oTs = [AT("oT0", [128, IC * 128], 0),
           AT("oT1", [128, IC * 128], 12 * KB)]
    scr3 = AT("scr3", [128, IC * 128], 24 * KB)
    a2os = [ATB("a2o0", [128, IC * 128], 36 * KB),
            ATB("a2o1", [128, IC * 128], 144 * KB)]
    b2os = [ATB("b2o0", [128, IC * 128], 42 * KB),
            ATB("b2o1", [128, IC * 128], 150 * KB)]
    out_sb = AT("out_sb", [128, D], 156 * KB)

    PA = lambda n, s: nc.alloc_psum_tensor(n, s, F32).ap()
    ptr8 = PA("ptr8", [128, D])
    pkw = PA("pkw", [128, D])
    pvw = PA("pvw", [128, D])
    pk0, pk1 = pkw[:, 0:512], pkw[:, 512:1024]
    pv0, pv1 = pvw[:, 0:512], pvw[:, 512:1024]
    pc = PA("pc", [128, 512])
    ps8 = PA("ps8", [128, 128])

    p = Prog(nc, es)

    MM = lambda out, l, r, st, sp: (lambda e: e.matmul(
        out, l, r, start=st, stop=sp, skip_group_check=True))
    TR = lambda out, in_: (lambda e: e.transpose(out, in_, ident))

    def transpose8(src, src_name, dst, dst_name):
        p.chain("pe", [TR(ptr8[:, k * 128:(k + 1) * 128],
                          src[:, k * 128:(k + 1) * 128]) for k in range(HC)],
                reads=[src_name, "ident"], writes=["ptr8"])
        p.op("dv", lambda e: e.tensor_copy(dst, ptr8),
             reads=["ptr8"], writes=[dst_name])

    def ln_stats(src_name, src, mo, io, mname, iname):
        p.op("dv", lambda e: e.bn_stats(ln6[:, 0:6], src[:, 0:512]),
             reads=[src_name], writes=["ln6"])
        p.op("dv", lambda e: e.bn_stats(ln6[:, 6:12], src[:, 512:1024]),
             reads=[src_name], writes=["ln6"])
        p.op("dv", lambda e: e.bn_aggr(lns2, ln6),
             reads=["ln6"], writes=["lns2"])
        p.op("ac", lambda e: e.activation(t128c, lns2[:, 1:2], AF.Ln,
                                          bias=epsc),
             reads=["lns2", "c_eps"], writes=["t_c"])
        p.op("ac", lambda e, io=io: e.activation(io, t128c, AF.Exp,
                                                 scale=-0.5),
             reads=["t_c"], writes=[iname])
        if mo is not None:
            p.op("dv", lambda e, mo=mo: e.tensor_copy(mo, lns2[:, 0:1]),
                 reads=["lns2"], writes=[mname])

    def exp_shift(wname, wbuf, dst, dstname):
        for kc in range(HC):
            for half in range(2):
                tgt = pk0 if half == 0 else pk1
                tn = "pk0" if half == 0 else "pk1"
                scr, scrn = ((sq_scr, "sq_scr") if half == 0
                             else (q_sbs[0][:, 0:512], "q_sb0"))
                p.op("ac", lambda e, kc=kc, half=half, scr=scr: e.activation(
                    scr, wbuf[:, kc, half * 512:(half + 1) * 512],
                    AF.Square),
                    reads=[wname], writes=[scrn])
                p.chain("pe", [MM(tgt[0:1, :], onesc, scr,
                                  kc == 0, kc == HC - 1)],
                        reads=[scrn, "c_onesc"], writes=[tn])
        p.op("dv", lambda e: e.reduce_max(s1x1[0:1, 0:1], pk0[0:1, :], AX.X),
             reads=["pk0"], writes=["s1x1"])
        p.op("dv", lambda e: e.reduce_max(s1x1[0:1, 1:2], pk1[0:1, :], AX.X),
             reads=["pk1"], writes=["s1x1"])
        p.op("dv", lambda e: e.tensor_max(s1x1[0:1, 2:3], s1x1[0:1, 0:1],
                                          s1x1[0:1, 1:2]),
             reads=["s1x1"], writes=["s1x1"])
        p.op("ac", lambda e: e.activation(s1x1[0:1, 3:4], s1x1[0:1, 2:3],
                                          AF.Ln, bias=epsc[0:1, :]),
             reads=["s1x1", "c_eps"], writes=["s1x1"])
        p.op("ac", lambda e: e.activation(s1x1[0:1, 4:5], s1x1[0:1, 3:4],
                                          AF.Exp, scale=0.5, bias=ln32),
             reads=["s1x1", "c_ln32"], writes=["s1x1"])
        p.chain("pe", [MM(ps8[:, 0:1], onesr, s1x1[0:1, 4:5], True, True)],
                reads=["s1x1", "c_onesr"], writes=["ps8"])
        p.op("dv", lambda e: e.tensor_scalar_mul(dst, ps8[:, 0:1], -1.0),
             reads=["ps8"], writes=[dstname])

    def round_act(buf, bufname, scale_ap, scale_read="scal"):
        p.op("ac", lambda e: e.activation(buf, buf, AF.Copy, scale=scale_ap,
                                          bias=MAGIC),
             reads=[bufname, scale_read], writes=[bufname])
        p.op("ac", lambda e: e.activation(buf, buf, AF.Copy, bias=-MAGIC),
             reads=[bufname], writes=[bufname])

    def round_rne(buf, bufname, scale_ap=None, scale_read=None):
        reads = [bufname] + ([scale_read] if scale_read else [])
        if scale_ap is not None:
            p.op("dv", lambda e: e.tensor_scalar(buf, buf, scale_ap, MAGIC,
                                                 ALU.mult, ALU.add),
                 reads=reads, writes=[bufname])
        else:
            p.op("dv", lambda e: e.tensor_scalar(buf, buf, MAGIC, None,
                                                 ALU.add, ALU.bypass),
                 reads=reads, writes=[bufname])
        p.op("dv", lambda e: e.tensor_scalar(buf, buf, MAGIC, None,
                                             ALU.subtract, ALU.bypass),
             reads=[bufname], writes=[bufname])

    p.op("dv", lambda e: e.memset(epsc, EPS), writes=["c_eps"])
    p.op("dv", lambda e: e.memset(ln32, float(np.log(np.float32(32.0 * 1.002)))),
         writes=["c_ln32"])
    p.op("dv", lambda e: e.memset(ctx, 0.0), writes=["ctx"])
    p.op("dv", lambda e: e.memset(gmax, 0.0), writes=["gmax"])
    p.op("dv", lambda e: e.memset(onesc_bf, 1.0), writes=["c_onesc_bf"])
    p.dma("qc", "sp", ident, ident_h, writes=["ident"])
    p.dma("qc", "sp", onesc, onesc_h, writes=["c_onesc"])
    p.dma("qc", "sp", onesr, onesr_h, writes=["c_onesr"])
    def load_w_pair(hbm, hbuf, lbuf, base):
        wslots = [(wstage[:, 0, :], "wstage0", []),
                  (wstage[:, 1, :], "wstage1", []),
                  (ef32, "ef32", []),
                  (wslot_x[0], "wkst0", ["e0h", "e1h"]),
                  (wslot_x[1], "wkst1", ["e0l", "e1l"]),
                  (wslot_x[2], "wkst2", ["v0h", "v1h"])]
        ns = len(wslots)
        for c in range(HC):
            ws, wsn, cov = wslots[c % ns]
            p.dma("qws%d" % (c % ns), "sp", ws,
                  hbm[c * 128:(c + 1) * 128, :], writes=[wsn] + cov)
            p.op("ac", lambda e, c=c, ws=ws: e.activation(
                hbuf[:, c, :], ws, AF.Copy),
                reads=[wsn], writes=[base + "h"])
            p.op("dv", lambda e, c=c, ws=ws: e.tensor_tensor(
                lbuf[:, c, :], ws, hbuf[:, c, :], ALU.subtract),
                reads=[wsn, base + "h"], writes=[base + "l"])

    p.dma("qc", "sp", bo_bc[0:1, :], bo_h, writes=["bo_bc"])
    for t0 in range(min(3, nT)):
        p.dma("qxt%d" % t0, "sp", xts[t0],
              x_in.rearrange("(t p) f -> p t f", p=128)[:, t0, :],
              writes=["xt%d" % t0])
    for half in range(2):
        tgt, tn = (pk0, "pk0") if half == 0 else (pk1, "pk1")
        p.chain("pe", [MM(tgt, onesr, bo_bc[0:1, half * 512:(half + 1) * 512],
                          True, True)],
                reads=["bo_bc", "c_onesr"], writes=[tn])
    p.op("dv", lambda e: e.tensor_copy(bo_bc[:, 0:512], pk0),
         reads=["pk0"], writes=["bo_bc"])
    p.op("dv", lambda e: e.tensor_copy(bo_bc[:, 512:1024], pk1),
         reads=["pk1"], writes=["bo_bc"])

    cbanks = [(pc, "pc"), (pv0, "pv0"), (pv1, "pv1"), (pk0, "pk0"),
              (pk1, "pk1")]

    def front_dma(t):
        px = t % 4
        p.dma("qxt%d" % px, "sp", xts[px],
              x_in.rearrange("(t p) f -> p t f", p=128)[:, t, :],
              writes=["xt%d" % px])

    def front_stats(t):
        par, px = t % 2, t % 4
        xn, hn = "xt%d" % px, "h%d" % par
        ln_stats(xn, xts[px], None, t128b, None, "t_inv")
        p.op("dv", lambda e, par=par, px=px: e.tensor_scalar(
            hs[par], xts[px], lns2[:, 0:1], t128b, ALU.subtract, ALU.mult),
            reads=[xn, "lns2", "t_inv"], writes=[hn])

    def front_tr(t):
        par = t % 2
        hn, hTn = "h%d" % par, "hT%d" % par
        p.chain("pe", [TR(ptr8[:, k * 128:(k + 1) * 128],
                          hs[par][:, k * 128:(k + 1) * 128])
                       for k in range(HC)],
                reads=[hn, "ident"], writes=["ptr8"])
        p.op("ac", lambda e, par=par: e.activation(hThs[par], ptr8, AF.Copy),
             reads=["ptr8"], writes=[hTn + "h"])
        p.op("dv", lambda e, par=par: e.tensor_tensor(
            hTls[par], ptr8, hThs[par], ALU.subtract),
            reads=["ptr8", hTn + "h"], writes=[hTn + "l"])

    def front(t):
        front_dma(t)
        front_stats(t)
        front_tr(t)

    def proj3(tgt, par, wh, wl, whn, wln, half):
        fns = []
        for kc in range(HC):
            sl = slice(kc * 128, (kc + 1) * 128)
            ws = (slice(None), kc, slice(half * 512, (half + 1) * 512))
            fns.append(MM(tgt, hThs[par][:, sl], wh[ws], kc == 0, False))
            fns.append(MM(tgt, hTls[par][:, sl], wh[ws], False, False))
            fns.append(MM(tgt, hThs[par][:, sl], wl[ws], False,
                          kc == HC - 1))
        return fns

    def backA(t):
        par = t % 2
        hTn = "hT%d" % par
        en, vn = "e%d" % par, "v%d" % par
        for half in range(2):
            tgt, tn = (pk0, "pk0") if half == 0 else (pk1, "pk1")
            p.chain("pe", proj3(tgt, par, w0h, w0l, "w0h", "w0l", half),
                    reads=[hTn + "h", hTn + "l", "w0h", "w0l"], writes=[tn])
        p.op("ac", lambda e: e.activation(ef32, pkw, AF.Exp, bias=negc),
             reads=["pk0", "pk1", "negc"], writes=["ef32"])
        p.op("ac", lambda e, par=par: e.activation(ehs[par], ef32, AF.Copy),
             reads=["ef32"], writes=[en + "h"])
        p.op("dv", lambda e, par=par: e.tensor_tensor(
            els[par], ef32, ehs[par], ALU.subtract),
            reads=["ef32", en + "h"], writes=[en + "l"])
        for half in range(2):
            tgt, tn = (pv0, "pv0") if half == 0 else (pv1, "pv1")
            p.chain("pe", proj3(tgt, par, w1h, w1l, "w1h", "w1l", half),
                    reads=[hTn + "h", hTn + "l", "w1h", "w1l"], writes=[tn])
        p.op("ac", lambda e, par=par: e.activation(vhs[par], pvw, AF.Copy),
             reads=["pv0", "pv1"], writes=[vn + "h"])
        p.op("dv", lambda e, par=par: e.tensor_tensor(
            vls[par], pvw, vhs[par], ALU.subtract),
            reads=["pv0", "pv1", vn + "h"], writes=[vn + "l"])

    def backA_ctx(t):
        par = t % 2
        en, vn = "e%d" % par, "v%d" % par
        for i in range(16):
            dc, half = i // 2, i % 2
            bank, bn = cbanks[i % len(cbanks)]
            dsl = slice(dc * 128, (dc + 1) * 128)
            hsl = slice(half * 512, (half + 1) * 512)
            p.chain("pe", [
                MM(bank, ehs[par][:, dsl], vhs[par][:, hsl], True, False),
                MM(bank, els[par][:, dsl], vhs[par][:, hsl], False, False),
                MM(bank, ehs[par][:, dsl], vls[par][:, hsl], False, True)],
                reads=[en + "h", en + "l", vn + "h", vn + "l"], writes=[bn])
            p.op("dv", lambda e, dc=dc, half=half, bank=bank: e.tensor_tensor(
                ctx[:, dc, half * 512:(half + 1) * 512],
                ctx[:, dc, half * 512:(half + 1) * 512], bank, ALU.add),
                reads=[bn, "ctx"], writes=["ctx"])
        fns = []
        for dc in range(HC):
            dsl = slice(dc * 128, (dc + 1) * 128)
            fns.append(MM(ps8[:, dc:dc + 1], ehs[par][:, dsl], onesc_bf,
                          True, False))
            fns.append(MM(ps8[:, dc:dc + 1], els[par][:, dsl], onesc_bf,
                          False, True))
        p.chain("pe", fns, reads=[en + "h", en + "l", "c_onesc_bf"],
                writes=["ps8"])
        p.op("dv", lambda e: e.tensor_tensor(
            ctx[:, :, 1024:1025], ctx[:, :, 1024:1025],
            ps8[:, 0:HC].unsqueeze(2), ALU.add),
            reads=["ps8", "ctx"], writes=["ctx"])

    front_stats(0)
    front_tr(0)
    if nT > 1:
        front_stats(1)
        front_tr(1)
    load_w_pair(wk_h, w0h, w0l, "w0")
    exp_shift("w0h", w0h, negc, "negc")
    load_w_pair(wv_h, w1h, w1l, "w1")
    for t in range(nT):
        if t + 2 < nT:
            front_stats(t + 2)
        backA(t)
        if t + 2 < nT:
            front_tr(t + 2)
        backA_ctx(t)
        if t + 3 < nT:
            front_dma(t + 3)

    p.fence()

    p.dma("qcc", "sp", cc_ctx_i, ctx, reads=["ctx"], writes=["cc_ctx_i"])
    p.op("gp", lambda e: e.collective_compute(
        "AllReduce", ALU.add, replica_groups=rg, ins=[cc_ctx_i],
        outs=[cc_ctx_o]), reads=["cc_ctx_i"], writes=["cc_ctx_o"])
    for t0b in range(min(3, nT)):
        front(t0b)
    load_w_pair(wq_h, w0h, w0l, "w0")
    exp_shift("w0h", w0h, negcq, "negcq")

    def emit_ctxwo():
        for dc in range(HC):
            p.chain("pe", [TR(ptr8[:, ec * 128:(ec + 1) * 128],
                              ctx[:, dc, ec * 128:(ec + 1) * 128])
                           for ec in range(HC)],
                    reads=["ctx", "ident"], writes=["ptr8"])
            p.op("ac", lambda e: e.activation(ctTh, ptr8, AF.Copy),
                 reads=["ptr8"], writes=["ctTh"])
            p.op("dv", lambda e: e.tensor_tensor(ctTl, ptr8, ctTh, ALU.subtract),
                 reads=["ptr8", "ctTh"], writes=["ctTl"])
            for half in range(2):
                tgt, tn = (pk0, "pk0") if half == 0 else (pk1, "pk1")
                fns = []
                for ec in range(HC):
                    sl = slice(ec * 128, (ec + 1) * 128)
                    ws = (slice(None), ec, slice(half * 512, (half + 1) * 512))
                    fns.append(MM(tgt, ctTh[:, sl], w1h[ws], ec == 0, False))
                    fns.append(MM(tgt, ctTl[:, sl], w1h[ws], False, False))
                    fns.append(MM(tgt, ctTh[:, sl], w1l[ws], False,
                                  ec == HC - 1))
                p.chain("pe", fns, reads=["ctTh", "ctTl", "w1h", "w1l"],
                        writes=[tn])
            p.op("ac", lambda e, dc=dc: e.activation(
                cWh[:, dc, :], pkw, AF.Copy, scale=rs8[:, dc:dc + 1]),
                reads=["pk0", "pk1", "rs8"], writes=["cWh"])
            p.op("dv", lambda e, dc=dc: e.scalar_tensor_tensor(
                cWl[:, dc, :], pkw, rs8[:, dc:dc + 1], cWh[:, dc, :],
                ALU.mult, ALU.subtract),
                reads=["pk0", "pk1", "rs8", "cWh"], writes=["cWl"])

    p.op("dv", lambda e: e.memset(gmax, 0.0), writes=["gmax"])

    def sq_mm(t):
        par = t % 2
        hTn, qn = "hT%d" % par, "q_sb%d" % par
        for half in range(2):
            tgt, tn = (pk0, "pk0") if half == 0 else (pk1, "pk1")
            p.chain("pe", proj3(tgt, par, w0h, w0l, "w0h", "w0l", half),
                    reads=[hTn + "h", hTn + "l", "w0h", "w0l"], writes=[tn])
        p.op("ac", lambda e, par=par: e.activation(
            q_sbs[par], pkw, AF.Exp, bias=negcq, accum_out=v128a),
            reads=["pk0", "pk1", "negcq"], writes=[qn, "v_a"])
        p.op("dv", lambda e: e.reciprocal(v128a, v128a),
             reads=["v_a"], writes=["v_a"])
        p.op("dv", lambda e, par=par: e.tensor_scalar_mul(
            q_sbs[par], q_sbs[par], v128a),
            reads=[qn, "v_a"], writes=[qn])

    def sq_tr(t):
        par = t % 2
        qn, qTn = "q_sb%d" % par, "qT%d" % par
        p.chain("pe", [TR(ptr8[:, k * 128:(k + 1) * 128],
                          q_sbs[par][:, k * 128:(k + 1) * 128])
                       for k in range(HC)],
                reads=[qn, "ident"], writes=["ptr8"])
        p.op("ac", lambda e, par=par: e.activation(qThs[par], ptr8, AF.Copy),
             reads=["ptr8"], writes=[qTn + "h"])
        p.op("dv", lambda e, par=par: e.tensor_tensor(
            qTls[par], ptr8, qThs[par], ALU.subtract),
            reads=["ptr8", qTn + "h"], writes=[qTn + "l"])

    def restB(t):
        par = t % 2
        qTn, x2n = "qT%d" % par, "x2t%d" % par
        for half in range(2):
            tgt, tn = (pv0, "pv0") if half == 0 else (pv1, "pv1")
            fns = []
            for dc in range(HC):
                sl = slice(dc * 128, (dc + 1) * 128)
                ws = (slice(None), dc, slice(half * 512, (half + 1) * 512))
                fns.append(MM(tgt, qThs[par][:, sl], cWh[ws], dc == 0, False))
                fns.append(MM(tgt, qTls[par][:, sl], cWh[ws], False, False))
                fns.append(MM(tgt, qThs[par][:, sl], cWl[ws], False,
                              dc == HC - 1))
            p.chain("pe", fns,
                    reads=[qTn + "h", qTn + "l", "cWh", "cWl"], writes=[tn])

    def restB_add(t):
        par = t % 2
        x2n = "x2t%d" % par
        p.op("dv", lambda e, par=par: e.tensor_tensor(
            x2ts[par], pvw, bo_bc, ALU.add),
            reads=["pv0", "pv1", "bo_bc"], writes=[x2n])
        p.op("dv", lambda e, par=par, px=t % 4: e.tensor_add(
            x2ts[par], x2ts[par], xts[px]),
            reads=[x2n, "xt%d" % (t % 4)], writes=[x2n])
        p.dma("qsx2", "gp", sp_x2[:, t, :], x2ts[par], reads=[x2n],
              writes=["sp_x2"])

    def restB_stats(t):
        par = t % 2
        x2n = "x2t%d" % par
        ln_stats(x2n, x2ts[par], stats[:, 2 * t:2 * t + 1],
                 stats[:, 2 * t + 1:2 * t + 2], "stats", "stats")
        p.op("dv", lambda e: e.tensor_copy(u128a, lns2[:, 0:1]),
             reads=["lns2"], writes=["u_mean"])
        p.op("dv", lambda e, par=par: e.reduce_max(t128c, x2ts[par], AX.X),
             reads=[x2n], writes=["t_c"])
        p.op("dv", lambda e, par=par: e.tensor_reduce(t128d, x2ts[par], AX.X,
                                                      ALU.min),
             reads=[x2n], writes=["t_d"])
        p.op("dv", lambda e: e.tensor_scalar(t128c, t128c, u128a, None,
                                             ALU.subtract, ALU.bypass),
             reads=["t_c", "u_mean"], writes=["t_c"])
        p.op("dv", lambda e: e.tensor_scalar(t128d, t128d, u128a, -1.0,
                                             ALU.subtract, ALU.mult),
             reads=["t_d", "u_mean"], writes=["t_d"])
        p.op("dv", lambda e: e.tensor_max(t128c, t128c, t128d),
             reads=["t_c", "t_d"], writes=["t_c"])
        p.op("dv", lambda e, t=t: e.tensor_mul(
            t128c, t128c, stats[:, 2 * t + 1:2 * t + 2]),
            reads=["t_c", "stats"], writes=["t_c"])
        p.op("dv", lambda e: e.tensor_max(gmax[:, 0:1], gmax[:, 0:1], t128c),
             reads=["t_c", "gmax"], writes=["gmax"])

    sq_mm(0)
    if nT > 1:
        sq_mm(1)
    sq_tr(0)
    if nT > 1:
        sq_tr(1)
    load_w_pair(wo_h, w1h, w1l, "w1")
    p.dma("qcc", "sp", ctx, cc_ctx_o, reads=["cc_ctx_o"], writes=["ctx"])
    for dc in range(HC):
        p.op("dv", lambda e, dc=dc: e.reciprocal(t128a, ctx[:, dc, 1024:1025]),
             reads=["ctx"], writes=["t_mean"])
        p.op("dv", lambda e, dc=dc: e.tensor_scalar_mul(
            rs8[:, dc:dc + 1], t128a, 0.125 / float(D)),
            reads=["t_mean"], writes=["rs8"])
    emit_ctxwo()
    if nT > 3:
        front_dma(3)
    for t in range(nT):
        if t + 3 < nT:
            front_stats(t + 3)
        restB(t)
        if t + 3 < nT:
            front_tr(t + 3)
        if t + 1 < nT and t > 0:
            sq_tr(t + 1)
        restB_add(t)
        if t + 2 < nT:
            sq_mm(t + 2)
        restB_stats(t)
        if t + 4 < nT:
            front_dma(t + 4)

    def scalar_allreduce(cols, k, nvals, pre_emit=None):
        for j, col in enumerate(cols):
            p.chain("pe", [TR(ptr8[0:1, 0:128], gmax[:, col:col + 1])],
                    reads=["gmax", "ident"], writes=["ptr8"])
            p.op("dv", lambda e: e.tensor_copy(srow, ptr8[0:1, 0:128]),
                 reads=["ptr8"], writes=["srow"])
            p.op("dv", lambda e, j=j: e.reduce_max(s1x1[0:1, j:j + 1], srow,
                                                   AX.X),
                 reads=["srow"], writes=["s1x1"])
        p.dma("qcc", "sp", cc_i[k][0:1, 0:nvals], s1x1[0:1, 0:nvals],
              reads=["s1x1"], writes=["cc_i%d" % k])
        p.op("gp", lambda e, k=k: e.collective_compute(
            "AllReduce", ALU.max, replica_groups=rg,
            ins=[cc_i[k][0:1, 0:nvals]], outs=[cc_o[k][0:1, 0:nvals]]),
            reads=["cc_i%d" % k], writes=["cc_o%d" % k])
        if pre_emit is not None:
            pre_emit()
        p.dma("qcc", "sp", s1x1[0:1, 0:nvals], cc_o[k][0:1, 0:nvals],
              reads=["cc_o%d" % k], writes=["s1x1"])

    def bcast_cols(c0, n):
        p.chain("pe", [MM(ps8[:, 0:n], onesr, m1x1[0:1, c0:c0 + n],
                          True, True)],
                reads=["m1x1", "c_onesr"], writes=["ps8"])
        p.op("dv", lambda e, c0=c0, n=n: e.tensor_copy(
            scal[:, c0:c0 + n], ps8[:, 0:n]),
            reads=["ps8"], writes=["scal"])

    def bcast_scale(src_col, rcol, scol, pre=None):
        if pre is not None:
            p.op("dv", lambda e: e.tensor_mul(
                s1x1[0:1, src_col:src_col + 1],
                s1x1[0:1, src_col:src_col + 1], m1x1[0:1, pre:pre + 1]),
                reads=["s1x1", "m1x1"], writes=["s1x1"])
        p.op("dv", lambda e: e.tensor_scalar(
            m1x1[0:1, scol:scol + 1], s1x1[0:1, src_col:src_col + 1],
            1.0 / QMAX, 1e-12, ALU.mult, ALU.max),
            reads=["s1x1"], writes=["m1x1"])
        p.op("dv", lambda e: e.reciprocal(m1x1[0:1, rcol:rcol + 1],
                                          m1x1[0:1, scol:scol + 1]),
             reads=["m1x1"], writes=["m1x1"])
        bcast_cols(min(rcol, scol), abs(scol - rcol) + 1)

    def fold_mul(dst, a, b):
        p.op("dv", lambda e: e.tensor_mul(
            m1x1[0:1, dst:dst + 1], m1x1[0:1, a:a + 1], m1x1[0:1, b:b + 1]),
            reads=["m1x1"], writes=["m1x1"])
        bcast_cols(dst, 1)

    def p2p_load(t):
        par = t % 2
        p.dma("qx2%d" % par, "sp", x2ts[par], sp_x2[:, t, :],
              reads=["sp_x2"], writes=["x2t%d" % par])
        p.op("dv", lambda e, t=t, par=par: e.tensor_scalar(
            hs[par], x2ts[par], stats[:, 2 * t:2 * t + 1],
            stats[:, 2 * t + 1:2 * t + 2], ALU.subtract, ALU.mult),
            reads=["x2t%d" % par, "stats"], writes=["h%d" % par])

    def p2p_main(t):
        par = t % 2
        hn = "h%d" % par
        round_act(hs[par], hn, scal[:, 0:1])
        p.chain("pe", [TR(ptr8[:, k * 128:(k + 1) * 128],
                          hs[par][:, k * 128:(k + 1) * 128])
                       for k in range(HC)],
                reads=[hn, "ident"], writes=["ptr8"])
        p.op("ac", lambda e: e.activation(p2scr, ptr8, AF.Copy,
                                          scale=1.0 / 256.0, bias=MAGIC),
             reads=["ptr8"], writes=["p2scr"])
        p.op("dv", lambda e, t=t: e.tensor_scalar(
            nA[:, :, t * 128:(t + 1) * 128],
            p2scr.rearrange("p (c k) -> p c k", k=128), MAGIC, 256.0,
            ALU.subtract, ALU.mult),
            reads=["p2scr"], writes=["nA"])
        p.op("dv", lambda e, t=t: e.scalar_tensor_tensor(
            nB[:, :, t * 128:(t + 1) * 128],
            nA[:, :, t * 128:(t + 1) * 128], -1.0,
            ptr8.rearrange("p (c k) -> p c k", k=128), ALU.mult, ALU.add),
            reads=["nA", "ptr8"], writes=["nB"])

    scalar_allreduce([0], 0, 1,
                     pre_emit=lambda: (p2p_load(0), p2p_load(1)))
    bcast_scale(0, 0, 1)

    p.fence()

    for t in range(nT):
        p2p_main(t)
        if t + 2 < nT:
            p2p_load(t + 2)

    def dequant_dve(nat, natname, scr, scrname, eng="dv", eng_rnd=None):
        if eng_rnd is None:
            eng_rnd = eng
        nb = nat.shape[-1] // 32
        bv = lambda a: a.rearrange("p (b k) -> p b k", k=32)
        p.op("dv", lambda e: e.tensor_reduce(
            blk[:, 0:nb], bv(nat), AX.X, ALU.max, apply_absolute_value=True),
            reads=[natname], writes=["blk"])
        p.op("dv", lambda e: e.tensor_scalar_mul(blk2[:, 0:nb], blk[:, 0:nb],
                                                 1.0 / WQMAX),
             reads=["blk"], writes=["blk2"])
        p.op("dv", lambda e: e.scalar_tensor_tensor(
            blk3[:, 0:nb], blk2[:, 0:nb], -WQMAX, blk[:, 0:nb],
            ALU.mult, ALU.add),
            reads=["blk", "blk2"], writes=["blk3"])
        p.op("dv", lambda e: e.scalar_tensor_tensor(
            blk2[:, 0:nb], blk3[:, 0:nb], 1.0 / WQMAX, blk2[:, 0:nb],
            ALU.mult, ALU.add),
            reads=["blk2", "blk3"], writes=["blk2"])
        p.op("dv", lambda e: e.tensor_scalar(blk2[:, 0:nb], blk2[:, 0:nb],
                                             1e-12, None, ALU.max, ALU.bypass),
             reads=["blk2"], writes=["blk2"])
        p.op("dv", lambda e: e.reciprocal(blk3[:, 0:nb], blk2[:, 0:nb]),
             reads=["blk2"], writes=["blk3"])
        p.op("dv", lambda e: e.tensor_mul(blk[:, 0:nb], blk3[:, 0:nb],
                                          blk2[:, 0:nb]),
             reads=["blk2", "blk3"], writes=["blk"])
        p.op("dv", lambda e: e.tensor_scalar(blk[:, 0:nb], blk[:, 0:nb],
                                             2.0, -1.0, ALU.subtract,
                                             ALU.mult),
             reads=["blk"], writes=["blk"])
        p.op("dv", lambda e: e.tensor_mul(blk3[:, 0:nb], blk3[:, 0:nb],
                                          blk[:, 0:nb]),
             reads=["blk", "blk3"], writes=["blk3"])
        p.op(eng, lambda e: e.tensor_tensor(
            bv(scr), bv(nat), blk3[:, 0:nb].unsqueeze(2)
            .broadcast_to([128, nb, 32]), ALU.mult),
            reads=[natname, "blk3"], writes=[scrname])
        p.op(eng_rnd, lambda e: e.tensor_scalar(scr, scr, MAGIC, MAGIC,
                                                ALU.add, ALU.subtract),
             reads=[scrname], writes=[scrname])
        p.op(eng, lambda e: e.tensor_tensor(
            bv(scr), bv(scr), blk2[:, 0:nb].unsqueeze(2)
            .broadcast_to([128, nb, 32]), ALU.mult),
            reads=[scrname, "blk2"], writes=[scrname])

    def dequant_tr(scr, scrname, wT_hi, wT_lo, wT_name):
        p.chain("pe", [TR(ptr8[:, k * 128:(k + 1) * 128],
                          scr[:, k * 128:(k + 1) * 128]) for k in range(HC)],
                reads=[scrname, "ident"], writes=["ptr8"])
        p.op("ac", lambda e: e.activation(wT_hi, ptr8, AF.Copy),
             reads=["ptr8"], writes=[wT_name + "h"])
        p.op("dv", lambda e: e.tensor_tensor(wT_lo, ptr8, wT_hi,
                                             ALU.subtract),
             reads=["ptr8", wT_name + "h"], writes=[wT_name + "l"])

    P2SPILL = [(sp_up, "sp_up", 1), (sp_gate, "sp_gate", 2)]
    p2banks = [(pk0, "pk0"), (pk1, "pk1"), (pv0, "pv0"), (pv1, "pv1")]
    p2cnt = [0]

    def p2_loads(jc):
        par = jc % 2
        p.dma("qwn%d0" % par, "sp", wnat[:, par, 0, :],
              wu_h[jc * 128:(jc + 1) * 128, :], writes=["wnat%d0" % par])
        p.dma("qwn%d1" % par, "sp", wnat[:, par, 1, :],
              wg_h[jc * 128:(jc + 1) * 128, :], writes=["wnat%d1" % par])

    def p2_dequant_dve(jc):
        par = jc % 2
        for wi in range(2):
            scr, scrn = (p2scr, "p2scr") if wi == 0 else (sq_scr2, "sq_scr2")
            dequant_dve(wnat[:, par, wi, :], "wnat%d%d" % (par, wi),
                        scr, scrn)

    def p2_dequant_tr(jc):
        par = jc % 2
        dequant_tr(p2scr, "p2scr", wTch[:, par, 0, :], wTcl[:, par, 0, :],
                   "wT%d0" % par)
        w1h_, w1l_ = wTch[:, par, 1, :], wTcl[:, par, 1, :]
        wn = "wT%d1" % par
        p.chain("pe", [TR(pc[:, k * 128:(k + 1) * 128],
                          sq_scr2[:, k * 128:(k + 1) * 128])
                       for k in range(4)],
                reads=["sq_scr2", "ident"], writes=["pc"])
        p.op("ac", lambda e, w1h_=w1h_: e.activation(
            w1h_[:, 0:512], pc, AF.Copy), reads=["pc"], writes=[wn + "h"])
        p.op("dv", lambda e, w1h_=w1h_, w1l_=w1l_: e.tensor_tensor(
            w1l_[:, 0:512], pc, w1h_[:, 0:512], ALU.subtract),
            reads=["pc", wn + "h"], writes=[wn + "l"])
        p.chain("pe", [TR(ptr8[:, k * 128:(k + 1) * 128],
                          sq_scr2[:, (4 + k) * 128:(5 + k) * 128])
                       for k in range(4)],
                reads=["sq_scr2", "ident"], writes=["ptr8"])
        p.op("ac", lambda e, w1h_=w1h_: e.activation(
            w1h_[:, 512:1024], ptr8[:, 0:512], AF.Copy),
            reads=["ptr8"], writes=[wn + "h"])
        p.op("dv", lambda e, w1h_=w1h_, w1l_=w1l_: e.tensor_tensor(
            w1l_[:, 512:1024], ptr8[:, 0:512], w1h_[:, 512:1024],
            ALU.subtract),
            reads=["ptr8", wn + "h"], writes=[wn + "l"])

    def p2_mms(jc):
        par = jc % 2
        sl2 = (jc % 2) * 2
        for wi, (spill, spname, gcol) in enumerate(P2SPILL):
            wtn = "wT%d%d" % (par, wi)
            stn = "stage%d" % (sl2 + wi)
            for st in range(nST):
                tgt, tn = p2banks[p2cnt[0] % 4]
                p2cnt[0] += 1
                fns = []
                for kc in range(HC):
                    for pi, (wt, na) in enumerate(
                            [(wTch, nA), (wTcl, nA), (wTch, nB)]):
                        fns.append(MM(
                            tgt[:, 0:STW],
                            wt[:, par, wi, kc * 128:(kc + 1) * 128],
                            na[:, kc, st * STW:(st + 1) * STW],
                            kc == 0 and pi == 0,
                            kc == HC - 1 and pi == 2))
                p.chain("pe", fns,
                        reads=[wtn + "h", wtn + "l", "nA", "nB"], writes=[tn])
                p.op("ac", lambda e, tgt=tgt, wi=wi, st=st, sl2=sl2:
                     e.activation(stage[:, sl2 + wi, st * STW:(st + 1) * STW],
                                  tgt[:, 0:STW], AF.Copy, scale=scal[:, 1:2]),
                     reads=[tn, "scal"], writes=[stn])
            p.op("dv", lambda e, wi=wi, sl2=sl2: e.tensor_reduce(
                t128e, stage[:, sl2 + wi, :], AX.X, ALU.max,
                apply_absolute_value=True),
                reads=[stn], writes=["t_e"])
            p.op("dv", lambda e, gcol=gcol: e.tensor_max(
                gmax[:, gcol:gcol + 1], gmax[:, gcol:gcol + 1], t128e),
                reads=["t_e", "gmax"], writes=["gmax"])
            p.dma("qsp%d" % wi, "gp", spill[jc, :, :], stage[:, sl2 + wi, :],
                  reads=[stn], writes=[spname])

    NWU = HC * (I3 // D)

    def wd_dma(u):
        hcp, piece, par = u // 3, u % 3, u % 2
        p.dma("qwd%d" % par, "sp", w4nat[:, par, :],
              wd_h[hcp * 128:(hcp + 1) * 128, piece * D:(piece + 1) * D],
              writes=["w4nat%d" % par])

    def wd_dq1(u):
        par = u % 2
        nat, natname = w4nat[:, par, :], "w4nat%d" % par
        scr, scrname = w4T[:, par, :], "w4T%d" % par
        bA, bB, bC = wblkA[par], wblkB[par], wblkC[par]
        An, Bn, Cn = "wblkA%d" % par, "wblkB%d" % par, "wblkC%d" % par
        bv = lambda a: a.rearrange("p (b k) -> p b k", k=32)
        p.op("dv", lambda e, nat=nat, bC=bC: e.tensor_reduce(
            bC, bv(nat), AX.X, ALU.max, apply_absolute_value=True),
            reads=[natname], writes=[Cn])
        p.op("dv", lambda e, bA=bA, bC=bC: e.tensor_scalar_mul(
            bA, bC, 1.0 / WQMAX), reads=[Cn], writes=[An])
        p.op("dv", lambda e, bA=bA, bB=bB, bC=bC: e.scalar_tensor_tensor(
            bB, bA, -WQMAX, bC, ALU.mult, ALU.add),
            reads=[An, Cn], writes=[Bn])
        p.op("dv", lambda e, bA=bA, bB=bB: e.scalar_tensor_tensor(
            bA, bB, 1.0 / WQMAX, bA, ALU.mult, ALU.add),
            reads=[An, Bn], writes=[An])
        p.op("dv", lambda e, bA=bA: e.tensor_scalar(
            bA, bA, 1e-12, None, ALU.max, ALU.bypass),
            reads=[An], writes=[An])
        p.op("dv", lambda e, bA=bA, bB=bB: e.reciprocal(bB, bA),
             reads=[An], writes=[Bn])
        p.op("dv", lambda e, bA=bA, bB=bB, bC=bC: e.tensor_mul(bC, bB, bA),
             reads=[An, Bn], writes=[Cn])
        p.op("dv", lambda e, bC=bC: e.tensor_scalar(
            bC, bC, 2.0, -1.0, ALU.subtract, ALU.mult),
            reads=[Cn], writes=[Cn])
        p.op("dv", lambda e, bB=bB, bC=bC: e.tensor_mul(bB, bB, bC),
             reads=[Bn, Cn], writes=[Bn])
        p.op("gp", lambda e, nat=nat, scr=scr, bB=bB: e.tensor_tensor(
            bv(scr), bv(nat), bB.unsqueeze(2).broadcast_to([128, 32, 32]),
            ALU.mult), reads=[natname, Bn], writes=[scrname])

    def wd_dq2(u):
        par = u % 2
        scr, scrname = w4T[:, par, :], "w4T%d" % par
        bA, An = wblkA[par], "wblkA%d" % par
        bv = lambda a: a.rearrange("p (b k) -> p b k", k=32)
        p.op("dv", lambda e, scr=scr: e.tensor_scalar(
            scr, scr, MAGIC, MAGIC, ALU.add, ALU.subtract),
            reads=[scrname], writes=[scrname])
        p.op("gp", lambda e, scr=scr, bA=bA: e.tensor_tensor(
            bv(scr), bv(scr), bA.unsqueeze(2).broadcast_to([128, 32, 32]),
            ALU.mult), reads=[scrname, An], writes=[scrname])

    def wd_spill(u):
        par = u % 2
        p.dma("qwq%d" % par, "sp", sp_wdq[u, :, :], w4T[:, par, :],
              reads=["w4T%d" % par], writes=["sp_wdq"])

    p2_loads(0)
    p2_loads(1)
    p2_dequant_dve(0)
    p2_dequant_tr(0)
    wd_dma(0)
    for jc in range(IC):
        if jc + 2 < IC:
            p2_loads(jc + 2)
        if jc + 1 < IC:
            p2_dequant_dve(jc + 1)
        p2_mms(jc)
        if jc + 1 < IC:
            p2_dequant_tr(jc + 1)
        if jc + 1 < NWU:
            wd_dma(jc + 1)
        wd_dq1(jc)
        wd_dq2(jc)
        wd_spill(jc)

    def wd_ld(u):
        par = u % 2
        p.dma("qwd%d" % par, "sp", w4T[:, par, :], sp_wdq[u, :, :],
              reads=["sp_wdq"], writes=["w4T%d" % par])

    def wd_tr(u):
        hcp, piece, par = u // 3, u % 3, u % 2
        p.chain("pe", [TR(ptr8[:, k * 128:(k + 1) * 128],
                          w4T[:, par, k * 128:(k + 1) * 128])
                       for k in range(HC)],
                reads=["w4T%d" % par, "ident"], writes=["ptr8"])
        dsth = wdTh[:, piece * HC:(piece + 1) * HC,
                    hcp * 128:(hcp + 1) * 128]
        dstl = wdTl[:, piece * HC:(piece + 1) * HC,
                    hcp * 128:(hcp + 1) * 128]
        pview = ptr8.rearrange("p (c k) -> p c k", k=128)
        p.op("ac", lambda e, dsth=dsth, pview=pview: e.activation(
            dsth, pview, AF.Copy),
            reads=["ptr8"], writes=["wdTh"])
        p.op("dv", lambda e, dsth=dsth, dstl=dstl, pview=pview:
             e.tensor_tensor(dstl, pview, dsth, ALU.subtract),
             reads=["ptr8", "wdTh"], writes=["wdTl"])

    def round_split(buf, bufname, scale_ap):
        p.op("ac", lambda e, buf=buf: e.activation(buf, buf, AF.Copy,
                                                   scale=scale_ap, bias=MAGIC),
             reads=[bufname, "scal"], writes=[bufname])
        p.op("dv", lambda e, buf=buf: e.tensor_scalar(buf, buf, MAGIC, None,
                                                      ALU.subtract,
                                                      ALU.bypass),
             reads=[bufname], writes=[bufname])

    p3_g = [(gstage[:, i, 0:T], "gst%d" % i) for i in range(3)]
    p3_g.append((ext0, "ext0"))

    def p3_dma(ic):
        g, gn = p3_g[ic % 4]
        p.dma("qg%d" % (ic % 4), "sp", g, sp_gate[ic, :, :],
              reads=["sp_gate"], writes=[gn, "nA", "nB"])

    def p3_round(ic):
        g, gn = p3_g[ic % 4]
        round_split(g, gn, scal[:, 4:5])

    def p3_chunk(ic):
        sl = ic % 3
        g, gn = p3_g[ic % 4]
        g2n = "gst%d" % (3 + sl)
        g2 = gstage[:, 3 + sl, 0:T]
        p.op("ac", lambda e, g=g, g2=g2: e.activation(
            g2, g, AF.Sigmoid, scale=scal[:, 5:6]),
            reads=[gn, "scal"], writes=[g2n])
        p.op("ac", lambda e, g2=g2: e.activation(g2, g2, AF.Copy,
                                                 scale=65536.0, bias=MAGIC),
             reads=[g2n], writes=[g2n])
        p.op("dv", lambda e, g2=g2: e.tensor_scalar(g2, g2, MAGIC, 65535.0,
                                                    ALU.subtract, ALU.min),
             reads=[g2n], writes=[g2n])
        p.op("gp", lambda e, g=g, g2=g2: e.tensor_mul(g2, g2, g),
             reads=[gn, g2n], writes=[g2n])
        p.dma("qspg2", "gp", sp_g2[ic, :, :], g2, reads=[g2n],
              writes=["sp_g2"])

    def p3_red(ic):
        sl = ic % 3
        g2n = "gst%d" % (3 + sl)
        g2 = gstage[:, 3 + sl, 0:T]
        p.op("dv", lambda e, g2=g2: e.tensor_reduce(
            t128e, g2, AX.X, ALU.max, apply_absolute_value=True),
            reads=[g2n], writes=["t_e"])
        p.op("dv", lambda e: e.tensor_max(gmax[:, 3:4], gmax[:, 3:4], t128e),
             reads=["t_e", "gmax"], writes=["gmax"])

    scalar_allreduce([1, 2], 1, 2, pre_emit=lambda: (
        p3_dma(0), p3_dma(1), p3_dma(2), wd_ld(0), wd_ld(1)))
    bcast_scale(0, 2, 3)
    bcast_scale(1, 4, 5)
    p.op("dv", lambda e: e.tensor_scalar_mul(m1x1[0:1, 12:13],
                                             m1x1[0:1, 5:6], 1.0 / 65536.0),
         reads=["m1x1"], writes=["m1x1"])

    p.fence()

    p3_round(0)
    p3_round(1)
    for ic in range(IC):
        p3_chunk(ic)
        wd_tr(ic)
        if ic + 2 < NWU:
            wd_ld(ic + 2)
        if ic > 0:
            p3_red(ic - 1)
        if ic + 2 < IC:
            p3_round(ic + 2)
        if ic + 3 < IC:
            p3_dma(ic + 3)
    p3_red(IC - 1)

    p4a_a = [(gstage[:, i, 0:T], "gst%d" % i) for i in range(3)]
    p4a_a.append((ext0, "ext0"))
    p4a_b = [(gstage[:, 3 + i, 0:T], "gst%d" % (3 + i)) for i in range(3)]
    p4a_b.append((ext1, "ext1"))

    def p4a_dma(ic):
        sl = ic % 4
        (a, an), (b, bn) = p4a_a[sl], p4a_b[sl]
        aw = [an] + (["w4nat0", "w4nat1"] if sl == 3 else [])
        bw = [bn] + (["w4T0", "w4T1"] if sl == 3 else [])
        p.dma("qg%d" % sl, "sp", a, sp_g2[ic, :, :], reads=["sp_g2"],
              writes=aw)
        p.dma("qh%d" % sl, "sp", b, sp_up[ic, :, :], reads=["sp_up"],
              writes=bw)

    def p4a_round(ic):
        sl = ic % 4
        (a, an), (b, bn) = p4a_a[sl], p4a_b[sl]
        round_act(a, an, scal[:, 10:11])
        round_rne(b, bn, scale_ap=scal[:, 2:3], scale_read="scal")

    def p4a_chunk(ic):
        sl = ic % 4
        (a, an), (b, bn) = p4a_a[sl], p4a_b[sl]
        p.op("gp", lambda e, a=a, b=b: e.tensor_mul(a, a, b),
             reads=[an, bn], writes=[an])
        p.dma("qspo", "gp", sp_o[ic, :, :], a, reads=[an], writes=["sp_o"])

    def p4a_red(ic):
        sl = ic % 4
        a, an = p4a_a[sl]
        p.op("dv", lambda e, a=a: e.tensor_reduce(
            t128e, a, AX.X, ALU.max, apply_absolute_value=True),
            reads=[an], writes=["t_e"])
        p.op("dv", lambda e: e.tensor_max(gmax[:, 4:5], gmax[:, 4:5], t128e),
             reads=["t_e", "gmax"], writes=["gmax"])

    scalar_allreduce([3], 2, 1, pre_emit=lambda: (
        p4a_dma(0), p4a_dma(1), p4a_dma(2), p4a_dma(3)))
    bcast_scale(0, 6, 7, pre=12)
    fold_mul(10, 12, 6)

    p4a_round(0)
    for ic in range(IC):
        p4a_chunk(ic)
        if ic > 0:
            p4a_red(ic - 1)
        if ic + 1 < IC:
            p4a_round(ic + 1)
        if ic + 3 < IC and ic >= 1:
            p4a_dma(ic + 3)
    p4a_red(IC - 1)

    def p4_loads(t):
        par = t % 2
        p.dma("qot%d" % par, "sp",
              oTs[par].rearrange("p (i c) -> p i c", c=128),
              sp_o.rearrange("i p (t c) -> p i t c", c=128)[:, :, t, :],
              reads=["sp_o"],
              writes=["oT%d" % par, "gst0", "gst1", "gst2"])
        p.dma("qx2%d" % par, "sp", x2ts[par], sp_x2[:, t, :],
              reads=["sp_x2"], writes=["x2t%d" % par])

    fold_mul(13, 7, 3)
    scalar_allreduce([4], 3, 1, pre_emit=lambda: (
        p4_loads(0), p4_loads(1)))
    bcast_scale(0, 8, 9, pre=13)
    fold_mul(11, 13, 8)

    p.fence()

    def p4_split(t):
        par = t % 2
        oTn = "oT%d" % par
        a2, a2n = a2os[par], "a2o%d" % par
        b2, b2n = b2os[par], "b2o%d" % par
        round_split(oTs[par], oTn, scal[:, 11:12])
        p.op("ac", lambda e, par=par: e.activation(
            scr3, oTs[par], AF.Copy, scale=1.0 / 256.0, bias=MAGIC),
            reads=[oTn], writes=["scr3"])
        p.op("dv", lambda e, a2=a2: e.tensor_scalar(a2, scr3, MAGIC, 256.0,
                                                    ALU.subtract, ALU.mult),
             reads=["scr3"], writes=[a2n])
        p.op("dv", lambda e, par=par, a2=a2, b2=b2: e.scalar_tensor_tensor(
            b2, a2, -1.0, oTs[par], ALU.mult, ALU.add),
            reads=[a2n, oTn], writes=[b2n])

    def p4_mm(t):
        par = t % 2
        x2n = "x2t%d" % par
        a2n, b2n = "a2o%d" % par, "b2o%d" % par
        pk_t = pkw if par == 0 else pvw
        tns = ["pk0", "pk1"] if par == 0 else ["pv0", "pv1"]
        a2v = a2os[par].rearrange("p (i c) -> p i c", c=128)
        b2v = b2os[par].rearrange("p (i c) -> p i c", c=128)
        for half in range(2):
            tgt = pk_t[:, half * 512:(half + 1) * 512]
            fns = []
            for ic in range(IC):
                for pi, (av, wt) in enumerate(
                        [(a2v, wdTh), (a2v, wdTl), (b2v, wdTh)]):
                    fns.append(MM(
                        tgt, av[:, ic, :],
                        wt[:, ic, half * 512:(half + 1) * 512],
                        ic == 0 and pi == 0, ic == IC - 1 and pi == 2))
            p.chain("pe", fns,
                    reads=[a2n, b2n, "wdTh", "wdTl"], writes=[tns[half]])
        p.op("dv", lambda e, par=par, pk_t=pk_t: e.scalar_tensor_tensor(
            out_sb, pk_t, scal[:, 9:10], x2ts[par], ALU.mult, ALU.add),
            reads=[tns[0], tns[1], "scal", x2n], writes=["out_sb"])
        p.dma("qo", "sp",
              y_out.rearrange("(t p) f -> p t f", p=128)[:, t, :], out_sb,
              reads=["out_sb"], writes=["y_done"])

    p4_split(0)
    for t in range(nT):
        if t + 1 < nT:
            p4_split(t + 1)
        p4_mm(t)
        if t + 2 < nT:
            p4_loads(t + 2)

    p.final_wait("sp", "qo")

    p.finalize()
    with nc.Block() as block:
        @block.sync
        def _(eng):
            p.replay("sp", eng)

        @block.tensor
        def _(eng):
            p.replay("pe", eng)

        @block.vector
        def _(eng):
            p.replay("dv", eng)

        @block.scalar
        def _(eng):
            p.replay("ac", eng)

        @block.gpsimd
        def _(eng):
            p.replay("gp", eng)

    return nc


def kernel(L0_attn_x, L0_attn_Wq, L0_attn_Wk, L0_attn_Wv, L0_attn_Wo,
           L0_attn_bo, L0_ffn_w_gate, L0_ffn_w_up, L0_ffn_w_down):
    world = N_CORES
    x = np.ascontiguousarray(np.asarray(L0_attn_x, dtype=np.float32)
                             .reshape(-1, D))
    Ttot = x.shape[0]
    T = Ttot // world
    base = {
        "wq": np.ascontiguousarray(L0_attn_Wq, dtype=np.float32),
        "wk": np.ascontiguousarray(L0_attn_Wk, dtype=np.float32),
        "wv": np.ascontiguousarray(L0_attn_Wv, dtype=np.float32),
        "wo": np.ascontiguousarray(L0_attn_Wo, dtype=np.float32),
        "bo": np.ascontiguousarray(np.asarray(L0_attn_bo, np.float32)
                                   .reshape(1, D)),
        "wgate": np.ascontiguousarray(L0_ffn_w_gate, dtype=np.float32),
        "wup": np.ascontiguousarray(L0_ffn_w_up, dtype=np.float32),
        "wdown": np.ascontiguousarray(L0_ffn_w_down, dtype=np.float32),
        "ident": np.eye(128, dtype=np.float32),
        "onesc": np.ones((128, 1), np.float32),
        "onesr": np.ones((1, 128), np.float32),
    }
    in_maps = []
    for c in range(world):
        m = dict(base)
        m["x"] = np.ascontiguousarray(x[c * T:(c + 1) * T])
        in_maps.append(m)
    nc = build(T, world)
    res = run_bass_kernel_spmd(nc, in_maps, core_ids=list(range(world)))
    out = np.concatenate([res.results[c]["y"] for c in range(world)], axis=0)
    return out.reshape(1, Ttot, D).astype(np.float32)
